# version 09
# baseline (speedup 1.0000x reference)
import contextlib
import numpy as np
import ml_dtypes
import concourse.bass as bass
import concourse.mybir as mybir
from concourse.bass_utils import run_bass_kernel_spmd

F32 = mybir.dt.float32
BF16 = mybir.dt.bfloat16
AF = mybir.ActivationFunctionType
ALU = mybir.AluOpType
AX = mybir.AxisListType

NCORES = 8
S = 8192
D = 2048
TOK = S // NCORES
NT = TOK // 128
PRIOR = S - TOK
DT = D // 128
JQ = 3 * D
DFF = 8192
F2 = 2 * DFF
DE = 1408
HT_E = DE // 128
DS = 2816
E = 8

LN_EPS = 1e-5
RMS_EPS = 1.1920929e-07
DEN_EPS = 1e-6
NEG = -1e30

P0_T = 4
A_T = 2
JCW = 256
JCQ = 512
HD = DT // 2

ENGINES = ("t", "v", "a", "g", "s")


class Prog:
    def __init__(self, nc, es, block):
        self.nc = nc
        self.es = es
        self.block = block
        self.pend = {e: [] for e in ENGINES}
        self.sems = {}
        self.hwm = {}
        self.floor = ()

    def _sem(self, name):
        if name not in self.sems:
            h = self.es.enter_context(self.nc.semaphore(name))
            self.sems[name] = [h, 0]
        return self.sems[name]

    def op(self, eng, fn, waits=(), inc=None, inc_k=1):
        merged = {}
        for ev in tuple(self.floor) + tuple(waits):
            if ev is None:
                continue
            name, tgt = ev
            if tgt <= 0:
                continue
            if merged.get(name, -1) < tgt:
                merged[name] = tgt
        real = []
        for name, tgt in merged.items():
            key = (eng, name)
            if self.hwm.get(key, -1) >= tgt:
                continue
            self.hwm[key] = tgt
            real.append((name, tgt))
        out_ev = None
        if inc is not None:
            rec = self._sem(inc)
            rec[1] += inc_k
            out_ev = (inc, rec[1])
        sems = self.sems

        def thunk(engine, real=tuple(real), fn=fn, inc=inc, inc_k=inc_k):
            for name, tgt in real:
                engine.wait_ge(sems[name][0], tgt)
            instr = fn(engine)
            if inc is not None:
                assert instr is not None, "inc op returned no instruction"
                instr.then_inc(sems[inc][0], inc_k)

        self.pend[eng].append(thunk)
        return out_ev

    def ev(self, name):
        return (name, self._sem(name)[1])

    def barrier(self):
        self.floor = tuple(self.ev(n) for n in self.sems)

    def flush(self):
        names = {"t": "tensor", "v": "vector", "a": "scalar",
                 "g": "gpsimd", "s": "sync"}
        for e in ENGINES:
            stream = self.pend[e]
            if not stream:
                continue
            self.pend[e] = []

            def run(engine, stream=stream):
                for th in stream:
                    th(engine)
            getattr(self.block, names[e])(run)


DEBUG_KEYS = ()


def build_nc(debug_keys=DEBUG_KEYS, no_p0=False):
    nc = bass.Bass("TRN2", target_bir_lowering=False)
    dbg = set(debug_keys)

    ap = {}
    def din(name, shape, dtype=F32):
        ap[name] = nc.dram_tensor(name, shape, dtype, kind="ExternalInput").ap()
    def dout(name, shape, dtype=F32):
        ap[name] = nc.dram_tensor(name, shape, dtype, kind="ExternalOutput").ap()
    def dint(name, shape, dtype=BF16):
        ap[name] = nc.dram_tensor(name, shape, dtype).ap()

    din("x_own", [TOK, D])
    din("x_prior", [PRIOR, D])
    din("maskb", [128, PRIOR // 128])
    din("w_qkv", [JQ, D])
    din("w_swiglu", [F2, D])
    din("w_out", [D, DFF])
    din("w_router_r", [128, DT * E])
    din("we_gate", [E, D, DE])
    din("we_up", [E, D, DE])
    din("we_down", [E, DE, D])
    din("ws_gate", [D, DS])
    din("ws_up", [D, DS])
    din("ws_down", [DS, D])
    din("identb", [128, 128], BF16)
    din("triu", [128, 128], BF16)
    din("e0row", [128, 128], F32)
    din("onesall", [128, 128], BF16)
    din("onescol_f", [128, 1], F32)

    dout("y", [TOK, D])
    for k, shape in (("carry", [2, D]), ("q", [TOK, D]), ("k", [TOK, D]),
                     ("w0", [128, D]), ("yy", [TOK, D]),
                     ("comb", [128, NT * E]), ("accC", [128, NT * D]),
                     ("cols", [128, 32]), ("xnT", [128, DT * A_T * 128])):
        if k in dbg:
            dout("dbg_" + k, shape)

    dint("s_qkv_nat", [JQ, D])
    dint("s_qkvT", [JQ // JCQ, 128, DT, JCQ])
    dint("s_swg_nat", [F2, D])
    dint("s_swgT", [F2 // JCW, 128, DT, JCW])
    dint("s_wout_nat", [D, DFF])
    dint("s_woutT", [DFF, D])

    es = contextlib.ExitStack()
    with es:
        block = es.enter_context(nc.Block())
        p = Prog(nc, es, block)

        def sbuf(name, shape, dtype, stack=None, side=None):
            return (stack or es).enter_context(
                nc.sbuf_tensor(name, shape, dtype, side=side))

        def psum(name, shape, stack=None):
            return (stack or es).enter_context(nc.psum_tensor(name, shape, F32))

        m_col = sbuf("m_col", [128, NT], F32, side="left")
        rstd_col = sbuf("rstd_col", [128, NT], F32, side="left")
        combine = sbuf("combine", [128, NT * E], F32, side="left")
        carry_w = sbuf("carry_w", [128, D], F32, side="left")
        carry_wv = sbuf("carry_wv", [128, D], F32, side="left")
        identb = sbuf("identb_b", [128, 128], BF16, side="left")
        triu_b = sbuf("triu_b", [128, 128], BF16, side="left")
        e0row_b = sbuf("e0row_b", [128, 128], F32, side="left")
        onesall_b = sbuf("onesall_b", [128, 128], BF16, side="left")
        onescol_fb = sbuf("onescol_fb", [128, 1], F32, side="left")
        maskb = sbuf("maskb_b", [128, PRIOR // 128], F32, side="left")
        wrb = sbuf("wrb", [128, DT * E], BF16, side="left")
        cvals = sbuf("cvals", [128, 8], F32, side="left")
        for idx, val in enumerate((LN_EPS, RMS_EPS, DEN_EPS, -0.5, -1.0,
                                   1.0 / D)):
            p.op("v", lambda v, idx=idx, val=val:
                 v.memset(cvals[:, idx:idx + 1], val))
            nc.const_aps.register(F32, val, cvals[:, idx:idx + 1])
        ev_cvals = p.op("v", lambda v: v.drain(), inc="v")
        p.floor = (ev_cvals,)

        ys = contextlib.ExitStack()
        yT = sbuf("yT", [128, DT * NT * 128], BF16, ys, side="right")
        ss_stk = contextlib.ExitStack()
        strip_buf = [sbuf(f"strip{i}", [128, 2048], BF16, ss_stk, side="right")
                     for i in range(2)]

        for dst, src in ((identb, "identb"), (triu_b, "triu"),
                         (e0row_b, "e0row"), (onesall_b, "onesall"),
                         (onescol_fb, "onescol_f"), (maskb, "maskb"),
                         (wrb, "w_router_r")):
            p.op("g", lambda g, dst=dst, src=src:
                 g.dma_start(out=dst[:], in_=ap[src]), inc="gc", inc_k=16)
        ev_consts = p.ev("gc")

        cast_ev = {}
        cast_ev["qkv_kv"] = p.op(
            "g", lambda g: g.dma_start(out=ap["s_qkv_nat"][D:JQ, :],
                                       in_=ap["w_qkv"][D:JQ, :]),
            inc="cast_qkv_kv", inc_k=16)
        cast_ev["qkv_q"] = p.op(
            "g", lambda g: g.dma_start(out=ap["s_qkv_nat"][0:D, :],
                                       in_=ap["w_qkv"][0:D, :]),
            inc="cast_qkv_q", inc_k=16)
        for tag, src, dst in (("swg", "w_swiglu", "s_swg_nat"),
                              ("wout", "w_out", "s_wout_nat")):
            cast_ev[tag] = p.op(
                "g", lambda g, src=src, dst=dst:
                g.dma_start(out=ap[dst], in_=ap[src]),
                inc="cast_" + tag, inc_k=16)

        jobs = []
        for hh in (1, 2):
            for ct in range(DT):
                jobs.append(("s_qkv_nat", "s_qkvT", ct, hh * 2048, 2048,
                             "qkv_kv"))
        for ct in range(DT):
            jobs.append(("s_qkv_nat", "s_qkvT", ct, 0, 2048, "qkv_q"))
        for ct in range(DT):
            for hh in range(8):
                jobs.append(("s_swg_nat", "s_swgT", ct, hh * 2048, 2048, "swg"))
        for ct in range(DFF // 128):
            jobs.append(("s_wout_nat", "s_woutT", ct, 0, 2048, "wout"))
        store_ev = [None, None]
        t_ready = {}
        for i, (nat, dstT, ct, r0, rn, tag) in enumerate(jobs):
            b = strip_buf[i % 2]
            qeng = "a" if (tag == "qkv_kv" and i % 2 == 1) else "s"
            tev = p.op(
                qeng,
                lambda sp, nat=nat, ct=ct, r0=r0, rn=rn, b=b: sp.dma_start(
                    out=b[:, 0:rn],
                    in_=ap[nat][r0:r0 + rn, ct * 128:(ct + 1) * 128],
                    transpose=True),
                waits=(cast_ev[tag], store_ev[i % 2]),
                inc=f"strip{i % 2}", inc_k=16)
            if tag == "wout":
                sev = p.op(
                    qeng,
                    lambda sp, dstT=dstT, ct=ct, r0=r0, rn=rn, b=b:
                    sp.dma_start(
                        out=ap[dstT][ct * 128:(ct + 1) * 128, r0:r0 + rn],
                        in_=b[:, 0:rn]),
                    waits=(tev,), inc=f"strip{i % 2}", inc_k=16)
            else:
                cw = JCQ if tag.startswith("qkv") else JCW
                for jl in range(rn // cw):
                    jc = (r0 + jl * cw) // cw
                    sev = p.op(
                        qeng,
                        lambda sp, dstT=dstT, ct=ct, jc=jc, jl=jl, b=b, cw=cw:
                        sp.dma_start(
                            out=ap[dstT][jc, :, ct, :],
                            in_=b[:, jl * cw:(jl + 1) * cw]),
                        waits=(tev,), inc=f"strip{i % 2}", inc_k=16)
            store_ev[i % 2] = sev
            t_ready[tag] = (p.ev("strip0"), p.ev("strip1"))
        kvT_ready = t_ready["qkv_kv"]
        qT_ready = t_ready["qkv_q"]
        swgT_ready = t_ready["swg"]
        woutT_ready = t_ready["wout"]
        p.flush()

        shs = contextlib.ExitStack()
        xin = sbuf("xin", [128, D], F32, shs, side="right")
        xn3 = [sbuf(f"xn3_{i}", [128, D], BF16, shs, side="right")
               for i in range(4)]
        xnT_a = sbuf("xnT_a", [128, DT * P0_T * 128], BF16, shs, side="right")
        kscr = sbuf("kscr", [128, P0_T * D], F32, shs, side="right")
        vscr = sbuf("vscr", [128, P0_T * D], BF16, shs, side="right")
        wslab = [sbuf(f"wslab{i}", [128, HD * JCQ], BF16, shs, side="right")
                 for i in range(2)]
        w_bf = sbuf("w_bf", [128, D], BF16, shs, side="right")
        wv_bf = sbuf("wv_bf", [128, D], BF16, shs, side="right")
        lncols = sbuf("lncols", [128, 8], F32, shs, side="right")
        sq_scr = sbuf("sq_scr", [128, D], F32, shs, side="right")
        srms8 = sbuf("srms8", [128, NT], F32, shs, side="right")
        ps_kv = [psum(f"ps_kv{i}", [128, 512], shs) for i in range(8)]
        p0s = contextlib.ExitStack()
        acc_w = sbuf("acc_w", [128, D], F32, p0s, side="right")
        acc_wv = sbuf("acc_wv", [128, D], F32, p0s, side="right")
        w_f = sq_scr

        p.op("v", lambda v: v.memset(acc_w[:], 0.0))
        p.op("v", lambda v: v.memset(acc_wv[:], 0.0))
        p.op("v", lambda v: v.memset(carry_w[:], 0.0))
        p.op("v", lambda v: v.memset(carry_wv[:], 0.0))
        p.op("v", lambda v: v.drain(), inc="v")

        st = {
            "xin_free": None,
            "xnf_free": [None, None, None, None],
            "slab_free": [None, None],
            "bank_free": [None] * 8,
            "wbf_free": None, "wvbf_free": None, "wff_free": None,
            "yf_free": None,
        }

        def ln_tile(xin, xev, xn_out, xn_w_wait, mm, rr, sscol):
            sums = lncols[:, 0:1]
            ss = lncols[:, 1:2]
            var = lncols[:, 2:3]
            t1 = lncols[:, 3:4]
            lnv = lncols[:, 4:5]
            p.op("v", lambda v: v.reduce_sum(sums[:], xin[:], axis=AX.X),
                 waits=(xev,))
            p.op("v", lambda v: v.drain())
            p.op("a", lambda a: a.activation(sq_scr[:], xin[:], AF.Square,
                                             accum_out=ss[:]),
                 waits=(xev, st["wff_free"]))
            aev = p.op("a", lambda a: a.drain(), inc="a")
            p.op("v", lambda v: v.tensor_scalar(mm[:], sums[:], 1.0 / D, None,
                                                ALU.mult))
            p.op("v", lambda v: v.drain())
            p.op("v", lambda v: v.tensor_mul(t1[:], mm[:], mm[:]), waits=(aev,))
            p.op("v", lambda v: v.drain())
            p.op("v", lambda v: v.scalar_tensor_tensor(
                var[:], ss[:], 1.0 / D, t1[:], ALU.mult, ALU.subtract))
            vev = p.op("v", lambda v: v.drain(), inc="v")
            p.op("a", lambda a: a.activation(lnv[:], var[:], AF.Ln,
                                             bias=LN_EPS), waits=(vev,))
            p.op("a", lambda a: a.drain())
            p.op("a", lambda a: a.activation(rr[:], lnv[:], AF.Exp, scale=-0.5))
            aev2 = p.op("a", lambda a: a.drain(), inc="a")
            p.op("v", lambda v: v.tensor_mul(t1[:], var[:], rr[:]),
                 waits=(aev2,))
            p.op("v", lambda v: v.drain())
            p.op("v", lambda v: v.tensor_mul(t1[:], t1[:], rr[:]))
            vev2 = p.op("v", lambda v: v.drain(), inc="v")
            p.op("a", lambda a: a.activation(lnv[:], t1[:], AF.Ln,
                                             bias=RMS_EPS), waits=(vev2,))
            p.op("a", lambda a: a.drain())
            p.op("a", lambda a: a.activation(sscol[:], lnv[:], AF.Exp,
                                             scale=-0.5))
            aev3 = p.op("a", lambda a: a.drain(), inc="a")
            p.op("v", lambda v: v.tensor_scalar(xn_out[:], xin[:], mm[:],
                                                rr[:], ALU.subtract, ALU.mult),
                 waits=(aev3, xn_w_wait))
            return p.op("v", lambda v: v.drain(), inc="v")

        def transpose_tile(src_bf, ev_src, dstT, tt, nt_cols, banks=(4, 5)):
            last = None
            for grp in range(4):
                b = banks[grp % 2]
                pev = p.op(
                    "t",
                    lambda t, grp=grp, b=b, src_bf=src_bf: [
                        t.matmul(ps_kv[b][:, k * 128:(k + 1) * 128],
                                 src_bf[:, (grp * 4 + k) * 128:
                                        (grp * 4 + k + 1) * 128],
                                 identb[:], start=True, stop=True)
                        for k in range(4)][-1],
                    waits=(ev_src, st["bank_free"][b], ev_consts),
                    inc="t")
                vev = p.op(
                    "v",
                    lambda v, grp=grp, b=b, tt=tt, dstT=dstT,
                           nt_cols=nt_cols: v.tensor_copy(
                        dstT[:, 0:DT * nt_cols * 128]
                        .rearrange("p (dt t) -> p dt t",
                                   dt=DT)[:, grp * 4:grp * 4 + 4,
                                          tt * 128:(tt + 1) * 128],
                        ps_kv[b][:, 0:512].rearrange(
                            "p (k c) -> p k c", k=4)) and v.drain(),
                    waits=(pev,), inc="v")
                st["bank_free"][b] = vev
                last = vev
            return last

        pcols = sbuf("pcols", [128, 8], F32, side="left")

        def prelude_tile(own, base_tok, gt, i):
            xsrc = "x_own" if own else "x_prior"
            xld = p.op(
                "g",
                lambda g, i=i, base_tok=base_tok, xsrc=xsrc: g.dma_start(
                    out=xin[:],
                    in_=ap[xsrc][base_tok + i * 128:
                                 base_tok + (i + 1) * 128, :]),
                waits=(st["xin_free"],), inc="xin0", inc_k=16)
            ti_own = (base_tok // 128) + i if own else i
            mm = m_col[:, ti_own:ti_own + 1] if own else lncols[:, 6:7]
            rr = (rstd_col[:, ti_own:ti_own + 1] if own
                  else lncols[:, 7:8])
            sc = srms8[:, i:i + 1]
            buf = xn3[i % 4]
            ev_xn = ln_tile(xin, xld, buf, st["xnf_free"][i % 4],
                            mm, rr, sc)
            st["xin_free"] = ev_xn
            return (ev_xn, buf, i % 4)

        def prelude_av(own, base_tok, gt):
            return [prelude_tile(own, base_tok, gt, i) for i in range(gt)]

        def prelude_t(pre, gt):
            tr_evs = []
            for i, (ev_xn, buf, slot3) in enumerate(pre):
                ev_tr = transpose_tile(buf, ev_xn, xnT_a, i, gt, (4, 5))
                st["xnf_free"][slot3] = p.ev("t")
                tr_evs.append(ev_tr)
            return tr_evs

        def qkv_mms(own, gt, v_gates, tr_evs, nxt_spec=None):
            nq = D // JCQ
            jcs = (list(range(0, 3 * nq)) if own
                   else list(range(nq, 3 * nq)))
            ev_q = [None] * gt
            ev_k = [None] * gt
            ev_v = [None] * gt
            hook_res = [None]
            if nxt_spec is not None:
                n_own, n_base, n_gt = nxt_spec
                hook_res[0] = []
                spc = max(1, len(jcs) // (n_gt + 1))
                hook_slots = {spc * (i + 1): i for i in range(n_gt)}
            for slot, jc in enumerate(jcs):
                if nxt_spec is not None and slot in hook_slots:
                    hook_res[0].append(
                        prelude_tile(n_own, n_base, n_gt,
                                     hook_slots[slot]))
                ready = qT_ready if jc < nq else kvT_ready
                sev = [None, None]
                for h in range(2):
                    sev[h] = p.op(
                        "g",
                        lambda g, jc=jc, h=h: g.dma_start(
                            out=wslab[h][:].rearrange("p (dt j) -> p dt j",
                                                      dt=HD),
                            in_=ap["s_qkvT"][jc][:, h * HD:(h + 1) * HD, :]),
                        waits=(st["slab_free"][h],) + ready,
                        inc=f"slab{h}", inc_k=16)
                mm_stop = [None] * gt
                for h in range(2):
                    for tt in range(gt):
                        mm_stop[tt] = p.op(
                            "t",
                            lambda t, tt=tt, h=h, gt=gt: [
                                t.matmul(ps_kv[tt][:, 0:JCQ],
                                         xnT_a[:, ((h * HD + d2) * gt + tt)
                                               * 128:
                                               ((h * HD + d2) * gt + tt)
                                               * 128 + 128],
                                         wslab[h][:, d2 * JCQ:
                                                  (d2 + 1) * JCQ],
                                         start=(h == 0 and d2 == 0),
                                         stop=(h == 1 and d2 == HD - 1))
                                for d2 in range(HD)][-1],
                            waits=(sev[h], tr_evs[tt], st["bank_free"][tt]),
                            inc="t")
                    st["slab_free"][h] = p.ev("t")
                for tt in range(gt):
                    mmev = mm_stop[tt]
                    if jc < nq:
                        dstbuf, col, dd = aonly["qscr"], jc * JCQ, ev_q
                    elif jc < 2 * nq:
                        dstbuf, col, dd = kscr, (jc - nq) * JCQ, ev_k
                    else:
                        dstbuf, col, dd = vscr, (jc - 2 * nq) * JCQ, ev_v
                    vg = v_gates[tt] if (dstbuf is vscr and v_gates) else None
                    if tt % 2 == 0:
                        p.op("a",
                             lambda a, tt=tt, dstbuf=dstbuf, col=col:
                             a.activation(
                                 dstbuf[:, tt * D + col: tt * D + col + JCQ],
                                 ps_kv[tt][:, 0:JCQ], AF.Copy,
                                 scale=srms8[:, tt:tt + 1]),
                             waits=(mmev, vg))
                        aev = p.op("a", lambda a: a.drain(), inc="a")
                    else:
                        p.op("v",
                             lambda v, tt=tt, dstbuf=dstbuf, col=col:
                             v.tensor_scalar(
                                 dstbuf[:, tt * D + col: tt * D + col + JCQ],
                                 ps_kv[tt][:, 0:JCQ],
                                 srms8[:, tt:tt + 1], None, ALU.mult),
                             waits=(mmev, vg))
                        aev = p.op("v", lambda v: v.drain(), inc="v")
                    st["bank_free"][tt] = aev
                    dd[tt] = aev
            return ev_q, ev_k, ev_v, hook_res[0]

        def p0_posts(hb, ev_k, ev_v):
            ss = pcols[:, 0:1]
            lnv = pcols[:, 1:2]
            invk = pcols[:, 2:3]
            gates = [None] * P0_T
            for i in range(P0_T):
                gtile = hb * P0_T + i
                p.op("a", lambda a, i=i: a.activation(
                        sq_scr[:], kscr[:, i * D:(i + 1) * D], AF.Square,
                        accum_out=ss[:]), waits=(ev_k[i], st["wff_free"]))
                p.op("a", lambda a: a.drain())
                p.op("a", lambda a: a.activation(lnv[:], ss[:], AF.Ln,
                                                 scale=1.0 / D, bias=RMS_EPS))
                p.op("a", lambda a: a.drain())
                p.op("a", lambda a: a.activation(invk[:], lnv[:], AF.Exp,
                                                 scale=-0.5))
                p.op("a", lambda a: a.drain())
                p.op("a", lambda a, i=i, gtile=gtile: a.activation(
                        w_f[:], kscr[:, i * D:(i + 1) * D], AF.Exp,
                        bias=maskb[:, gtile:gtile + 1], scale=invk[:]),
                     waits=(ev_consts,))
                aev = p.op("a", lambda a: a.drain(), inc="a")
                p.op("v", lambda v: v.tensor_add(acc_w[:], acc_w[:], w_f[:]),
                     waits=(aev,))
                p.op("v", lambda v: v.drain())
                p.op("v", lambda v, i=i: v.tensor_mul(
                        w_f[:], w_f[:], vscr[:, i * D:(i + 1) * D]),
                     waits=(ev_v[i],))
                gates[i] = p.op("v", lambda v: v.drain(), inc="v")
                p.op("v", lambda v: v.tensor_add(acc_wv[:], acc_wv[:],
                                                 w_f[:]))
                st["wff_free"] = p.op("v", lambda v: v.drain(), inc="v")
            return gates

        def a_posts(qb, ev_q, ev_k, ev_v):
            ss = pcols[:, 0:1]
            lnv = pcols[:, 1:2]
            invk = pcols[:, 2:3]
            invq2 = pcols[:, 3:5]
            gates = [None] * A_T
            if "q" in dbg:
                for i in range(A_T):
                    gi = qb * A_T + i
                    p.op("g", lambda sp, i=i, gi=gi: sp.dma_start(
                            out=ap["dbg_q"][gi * 128:(gi + 1) * 128, :],
                            in_=qscr[:, i * D:(i + 1) * D]),
                         waits=(ev_q[i],), inc="dbg", inc_k=16)
            if "k" in dbg:
                for i in range(A_T):
                    gi = qb * A_T + i
                    p.op("g", lambda sp, i=i, gi=gi: sp.dma_start(
                            out=ap["dbg_k"][gi * 128:(gi + 1) * 128, :],
                            in_=kscr[:, i * D:(i + 1) * D]),
                         waits=(ev_k[i],), inc="dbg", inc_k=16)
            if "xnT" in dbg and qb == 0:
                p.op("g", lambda sp: sp.dma_start(
                        out=ap["dbg_xnT"], in_=xnT_a[:, 0:DT * A_T * 128]),
                     waits=(p.ev("v"),), inc="dbg", inc_k=16)
            if dbg:
                p.barrier()
            for i in range(A_T):
                gi = qb * A_T + i
                p.op("a", lambda a, i=i: a.activation(
                        sq_scr[:], qscr[:, i * D:(i + 1) * D], AF.Square,
                        accum_out=ss[:]), waits=(ev_q[i],))
                p.op("a", lambda a: a.drain())
                p.op("a", lambda a: a.activation(lnv[:], ss[:], AF.Ln,
                                                 scale=1.0 / D, bias=RMS_EPS))
                p.op("a", lambda a: a.drain())
                p.op("a", lambda a, i=i: a.activation(
                        invq2[:, i:i + 1], lnv[:], AF.Exp, scale=-0.5))
                p.op("a", lambda a: a.drain())
                p.op("a", lambda a, i=i: a.activation(
                        sq_scr[:], kscr[:, i * D:(i + 1) * D], AF.Square,
                        accum_out=ss[:]), waits=(ev_k[i],))
                p.op("a", lambda a: a.drain())
                p.op("a", lambda a: a.activation(lnv[:], ss[:], AF.Ln,
                                                 scale=1.0 / D, bias=RMS_EPS))
                p.op("a", lambda a: a.drain())
                p.op("a", lambda a: a.activation(invk[:], lnv[:], AF.Exp,
                                                 scale=-0.5))
                p.op("a", lambda a: a.drain())
                p.op("a", lambda a, i=i: a.activation(
                        w_bf[:], kscr[:, i * D:(i + 1) * D], AF.Exp,
                        scale=invk[:]), waits=(st["wbf_free"],))
                aev = p.op("a", lambda a: a.drain(), inc="a")
                if "w0" in dbg and gi == 0:
                    p.op("g", lambda sp: sp.dma_start(out=ap["dbg_w0"],
                                                      in_=w_bf[:]),
                         waits=(aev,), inc="dbg", inc_k=16)
                    p.barrier()
                p.op("v", lambda v, i=i: v.tensor_mul(
                        wv_bf[:], w_bf[:], vscr[:, i * D:(i + 1) * D]),
                     waits=(aev, ev_v[i], st["wvbf_free"]))
                vev_wv = p.op("v", lambda v: v.drain(), inc="v")
                gates[i] = vev_wv
                rscr_free = [None, None]
                for ch in range(4):
                    bA = 4 + (ch % 2) * 2
                    bB = 5 + (ch % 2) * 2
                    rs = r_scr[:, (ch % 2) * 512:(ch % 2) * 512 + 512]
                    mmA = p.op(
                        "t",
                        lambda t, ch=ch, bA=bA: (
                            t.matmul(ps_kv[bA][:], triu_b[:],
                                     w_bf[:, ch * 512:(ch + 1) * 512],
                                     start=True, stop=False),
                            t.matmul(ps_kv[bA][:], e0row_b[:],
                                     carry_w[:, ch * 512:(ch + 1) * 512],
                                     start=False, stop=True))[-1],
                        waits=(aev, st["bank_free"][bA], p.ev("v")),
                        inc="t")
                    mmB = p.op(
                        "t",
                        lambda t, ch=ch, bB=bB: (
                            t.matmul(ps_kv[bB][:], triu_b[:],
                                     wv_bf[:, ch * 512:(ch + 1) * 512],
                                     start=True, stop=False),
                            t.matmul(ps_kv[bB][:], e0row_b[:],
                                     carry_wv[:, ch * 512:(ch + 1) * 512],
                                     start=False, stop=True))[-1],
                        waits=(vev_wv, st["bank_free"][bB]),
                        inc="t")
                    p.op("a", lambda a, bA=bA, rs=rs: a.activation(
                            rs[:], ps_kv[bA][:], AF.Ln, bias=DEN_EPS),
                         waits=(mmA, rscr_free[ch % 2]))
                    p.op("a", lambda a: a.drain())
                    p.op("a", lambda a, rs=rs: a.activation(
                            rs[:], rs[:], AF.Exp, scale=-1.0))
                    aevr = p.op("a", lambda a: a.drain(), inc="a")
                    st["bank_free"][bA] = aevr
                    vev = p.op(
                        "v",
                        lambda v, ch=ch, bB=bB, rs=rs:
                        v.tensor_tensor(
                            y_f[:, ch * 512:(ch + 1) * 512],
                            ps_kv[bB][:], rs[:], ALU.mult) and v.drain(),
                        waits=(mmB, aevr, st["yf_free"]), inc="v")
                    st["bank_free"][bB] = vev
                    rscr_free[ch % 2] = vev
                for src, dstrow, wsrc in ((w_bf, carry_w, aev),
                                          (wv_bf, carry_wv, vev_wv)):
                    for ch in range(4):
                        bank = 4 + ((ch + (0 if src is w_bf else 1)) % 4)
                        mmev = p.op(
                            "t",
                            lambda t, src=src, ch=ch, bank=bank: t.matmul(
                                ps_kv[bank][:], onesall_b[:],
                                src[:, ch * 512:(ch + 1) * 512],
                                start=True, stop=True),
                            waits=(st["bank_free"][bank], wsrc, ev_consts),
                            inc="t")
                        vev = p.op(
                            "v",
                            lambda v, dstrow=dstrow, ch=ch, bank=bank:
                            v.tensor_add(
                                dstrow[0:1, ch * 512:(ch + 1) * 512],
                                dstrow[0:1, ch * 512:(ch + 1) * 512],
                                ps_kv[bank][0:1, :]) and v.drain(),
                            waits=(mmev,), inc="v")
                        st["bank_free"][bank] = vev
                p.op("v", lambda v: v.drain(), inc="v")
                st["wbf_free"] = p.ev("t")
                st["wvbf_free"] = p.ev("t")
                aev = p.op(
                    "a",
                    lambda a, i=i: a.activation(
                        sigq_t[:], qscr[:, i * D:(i + 1) * D], AF.Sigmoid,
                        scale=invq2[:, i:i + 1]) and a.drain(),
                    inc="a")
                vev = p.op(
                    "v",
                    lambda v: v.tensor_mul(y_f[:], y_f[:],
                                           sigq_t[:]) and v.drain(),
                    waits=(aev,), inc="v")
                transpose_tile(y_f, vev, yT, gi, NT, (2, 3))
                st["yf_free"] = p.ev("t")
                if "yy" in dbg:
                    p.op("g", lambda sp, gi=gi: sp.dma_start(
                            out=ap["dbg_yy"][gi * 128:(gi + 1) * 128, :],
                            in_=y_f[:]),
                         waits=(vev,), inc="dbg", inc_k=16)
                    p.barrier()
            return gates

        aos = contextlib.ExitStack()
        aonly = {}

        nP0 = 0 if no_p0 else PRIOR // (P0_T * 128)
        post_gate = None
        if nP0:
            tr_evs = prelude_t(prelude_av(False, 0, P0_T), P0_T)
        for hb in range(nP0):
            if hb + 1 < nP0:
                spec = (False, (hb + 1) * P0_T * 128, P0_T)
                ngt = P0_T
            else:
                spec = (True, 0, A_T)
                ngt = A_T
            _, ev_k, ev_v, nxt = qkv_mms(False, P0_T, post_gate, tr_evs,
                                         spec)
            tr_evs = prelude_t(nxt, ngt)
            post_gate = p0_posts(hb, ev_k, ev_v)
        if nP0 == 0:
            tr_evs = prelude_t(prelude_av(True, 0, A_T), A_T)

        accred_ev = p.ev("v")
        for src, dst in ((acc_w, carry_w), (acc_wv, carry_wv)):
            for ch in range(4):
                bank = ch
                mmev = p.op(
                    "t",
                    lambda t, src=src, ch=ch, bank=bank: t.matmul(
                        ps_kv[bank][0:1, :], onescol_fb[:],
                        src[:, ch * 512:(ch + 1) * 512],
                        start=True, stop=True),
                    waits=(accred_ev, st["bank_free"][bank], ev_consts),
                    inc="t")
                vev = p.op(
                    "v",
                    lambda v, dst=dst, ch=ch, bank=bank: v.tensor_copy(
                        dst[0:1, ch * 512:(ch + 1) * 512],
                        ps_kv[bank][0:1, :]) and v.drain(),
                    waits=(mmev,), inc="v")
                st["bank_free"][bank] = vev
        carry_ev = p.ev("v")
        p.flush()
        p.barrier()
        p0s.close()
        aonly["qscr"] = sbuf("qscr", [128, A_T * D], BF16, aos, side="right")
        qscr = aonly["qscr"]
        y_f = sbuf("y_f", [128, D], BF16, aos, side="right")
        sigq_t = sbuf("sigq_t", [128, D], F32, aos, side="right")
        r_scr = sbuf("r_scr", [128, 1024], F32, aos, side="right")
        if "carry" in dbg:
            p.op("g", lambda sp: sp.dma_start(out=ap["dbg_carry"][0:1, :],
                                              in_=carry_w[0:1, :]),
                 waits=(carry_ev,), inc="dbg", inc_k=16)
            p.op("g", lambda sp: sp.dma_start(out=ap["dbg_carry"][1:2, :],
                                              in_=carry_wv[0:1, :]),
                 waits=(carry_ev,), inc="dbg", inc_k=16)
            p.barrier()

        nA = TOK // (A_T * 128)
        gateA = None
        for qb in range(nA):
            spec = ((True, (qb + 1) * A_T * 128, A_T)
                    if qb + 1 < nA else None)
            ev_q, ev_k, ev_v, nxt = qkv_mms(True, A_T, gateA, tr_evs, spec)
            if nxt is not None:
                tr_evs = prelude_t(nxt, A_T)
            gateA = a_posts(qb, ev_q, ev_k, ev_v)
        yT_ev = p.ev("v")
        p.flush()
        p.barrier()
        aos.close()
        shs.close()
        ss_stk.close()

        acc = sbuf("acc", [128, NT * D], F32, side="left")
        cs = contextlib.ExitStack()
        uslab = [sbuf(f"uslab{i}", [128, DT * 256], BF16, cs, side="right")
                 for i in range(2)]
        gslab = [sbuf(f"gslab{i}", [128, DT * 256], BF16, cs, side="right")
                 for i in range(2)]
        zbuf = [sbuf(f"zbuf{i}", [128, TOK], BF16, cs, side="right")
                for i in range(8)]
        silg = sbuf("silg", [128, TOK], F32, cs, side="right")
        wo_ch = [sbuf(f"wo_ch{i}", [128, 512], BF16, cs, side="right")
                 for i in range(16)]
        ps_u = psum("ps_u", [128, 1024], cs)
        ps_g = psum("ps_g", [128, 1024], cs)
        ps_wo = [psum(f"ps_wo{i}", [128, 512], cs) for i in range(2)]

        cst = {"uslab_free": [None, None], "gslab_free": [None, None],
               "wo_free": [None] * 16, "zfree": [None] * 8,
               "psu_free": None, "psg_free": None, "silg_free": None,
               "pswo_free": [None, None], "uld": None, "gld": None}
        z_ev = [None] * 8
        first_group = True
        for fg in range(8):
            for fi in range(8):
                f = fg * 8 + fi
                par = (f // 2) % 2
                if fi % 2 == 0:
                    cst["uld"] = p.op(
                        "g",
                        lambda g, f=f, par=par: g.dma_start(
                            out=uslab[par][:].rearrange(
                                "p (dt j) -> p dt j", dt=DT),
                            in_=ap["s_swgT"][f // 2]),
                        waits=swgT_ready + (cst["uslab_free"][par],),
                        inc=f"usl{par}", inc_k=16)
                    cst["gld"] = p.op(
                        "g",
                        lambda g, f=f, par=par: g.dma_start(
                            out=gslab[par][:].rearrange(
                                "p (dt j) -> p dt j", dt=DT),
                            in_=ap["s_swgT"][DFF // JCW + f // 2]),
                        waits=(cst["gslab_free"][par],),
                        inc=f"gsl{par}", inc_k=16)
                off = (f % 2) * 128
                ev_u = p.op(
                    "t",
                    lambda t, par=par, off=off: [
                        t.matmul(ps_u[:, tch * 512:(tch + 1) * 512],
                                 uslab[par][:, dt * 256 + off:
                                            dt * 256 + off + 128],
                                 yT[:, (dt * NT + tch * 4) * 128:
                                    (dt * NT + tch * 4) * 128 + 512],
                                 start=(dt == 0), stop=(dt == DT - 1))
                        for dt in range(DT) for tch in range(2)][-1],
                    waits=(cst["uld"], yT_ev, cst["psu_free"]),
                    inc="t")
                ev_g = p.op(
                    "t",
                    lambda t, par=par, off=off: [
                        t.matmul(ps_g[:, tch * 512:(tch + 1) * 512],
                                 gslab[par][:, dt * 256 + off:
                                            dt * 256 + off + 128],
                                 yT[:, (dt * NT + tch * 4) * 128:
                                    (dt * NT + tch * 4) * 128 + 512],
                                 start=(dt == 0), stop=(dt == DT - 1))
                        for dt in range(DT) for tch in range(2)][-1],
                    waits=(cst["gld"], cst["psg_free"]),
                    inc="t")
                if fi % 2 == 1:
                    cst["uslab_free"][par] = p.ev("t")
                    cst["gslab_free"][par] = p.ev("t")
                aev = p.op(
                    "a",
                    lambda a: [a.activation(silg[:, tch * 512:(tch + 1) * 512],
                                            ps_g[:, tch * 512:(tch + 1) * 512],
                                            AF.Silu)
                               for tch in range(2)][-1] and a.drain(),
                    waits=(ev_g, cst["silg_free"]), inc="a")
                cst["psg_free"] = aev
                vev = p.op(
                    "v",
                    lambda v, fi=fi: [v.scalar_tensor_tensor(
                        zbuf[fi][:, tch * 512:(tch + 1) * 512],
                        ps_u[:, tch * 512:(tch + 1) * 512], 1.0,
                        silg[:, tch * 512:(tch + 1) * 512],
                        ALU.mult, ALU.mult) for tch in range(2)][-1]
                    and v.drain(),
                    waits=(ev_u, aev, cst["zfree"][fi]), inc="v")
                cst["psu_free"] = vev
                cst["silg_free"] = vev
                z_ev[fi] = vev
            for ddc in range(4):
                half = (ddc % 2) * 8
                wo_ld = []
                for fi in range(8):
                    f = fg * 8 + fi
                    wo_ld.append(p.op(
                        "s",
                        lambda sp, f=f, fi=fi, ddc=ddc, half=half:
                        sp.dma_start(
                            out=wo_ch[half + fi][:],
                            in_=ap["s_woutT"][f * 128:(f + 1) * 128,
                                              ddc * 512:(ddc + 1) * 512]),
                        waits=woutT_ready + (cst["wo_free"][half + fi],),
                        inc=f"wo{ddc % 2}", inc_k=16))
                for tt in range(NT):
                    slot = (tt * 4 + ddc) % 2
                    mmev = p.op(
                        "t",
                        lambda t, tt=tt, ddc=ddc, slot=slot, half=half: [
                            t.matmul(ps_wo[slot][:],
                                     zbuf[fi2][:, tt * 128:(tt + 1) * 128],
                                     wo_ch[half + fi2][:],
                                     start=(fi2 == 0), stop=(fi2 == 7))
                            for fi2 in range(8)][-1],
                        waits=tuple(z_ev) + tuple(wo_ld)
                              + (cst["pswo_free"][slot],),
                        inc="t")
                    if first_group:
                        vev = p.op(
                            "v",
                            lambda v, tt=tt, ddc=ddc, slot=slot:
                            v.tensor_copy(
                                acc[:, tt * D + ddc * 512:
                                    tt * D + (ddc + 1) * 512],
                                ps_wo[slot][:]) and v.drain(),
                            waits=(mmev,), inc="v")
                    else:
                        vev = p.op(
                            "v",
                            lambda v, tt=tt, ddc=ddc, slot=slot:
                            v.tensor_add(
                                acc[:, tt * D + ddc * 512:
                                    tt * D + (ddc + 1) * 512],
                                acc[:, tt * D + ddc * 512:
                                    tt * D + (ddc + 1) * 512],
                                ps_wo[slot][:]) and v.drain(),
                            waits=(mmev,), inc="v")
                    cst["pswo_free"][slot] = vev
                woe = p.ev("t")
                for fi in range(8):
                    cst["wo_free"][half + fi] = woe
            first_group = False
            ze = p.ev("t")
            for fi in range(8):
                cst["zfree"][fi] = ze
        if "accC" in dbg:
            p.op("g", lambda sp: sp.dma_start(out=ap["dbg_accC"], in_=acc[:]),
                 waits=(p.ev("v"),), inc="dbg", inc_k=16)
        p.flush()
        p.barrier()
        cs.close()
        ys.close()

        bs = contextlib.ExitStack()
        xnT_b = sbuf("xnT_bb", [128, DT * NT * 128], BF16, bs, side="right")
        xin_b = sbuf("xin_b", [128, D], F32, bs, side="right")
        xn_fb = sbuf("xn_fb", [128, D], BF16, bs, side="right")
        hh = [sbuf(f"hh{i}", [128, TOK], BF16, bs, side="right")
              for i in range(HT_E)]
        wgc = [sbuf(f"wgc{i}", [128, DT * 128], BF16, bs, side="right")
               for i in range(2)]
        wuc = [sbuf(f"wuc{i}", [128, DT * 128], BF16, bs, side="right")
               for i in range(2)]
        wdc = [sbuf(f"wdc{i}", [128, D], BF16, bs, side="right")
               for i in range(6)]
        silh = sbuf("silh", [128, TOK], F32, bs, side="right")
        rcols = sbuf("rcols", [128, 8], F32, bs, side="right")
        scores = sbuf("scores", [128, E], F32, bs, side="right")
        emask = sbuf("emask", [128, E], F32, bs, side="right")
        ps_hg = psum("ps_hg", [128, 1024], bs)
        ps_hu = psum("ps_hu", [128, 1024], bs)
        ps_dn = [psum(f"ps_dn{i}", [128, 512], bs) for i in range(2)]
        ps_tr2 = [psum(f"ps_tr2{i}", [128, 512], bs) for i in range(2)]
        ps_r = ps_dn[0]

        bst = {"xin_free": None, "xnf_free": None,
               "tr_free": [None, None],
               "wg_free": [None, None], "wu_free": [None, None],
               "wd_free": [None] * 6, "hh_free": [None] * HT_E,
               "hg_free": None, "hu_free": None, "silh_free": None,
               "dn_free": [None, None]}
        hh_ev = [None] * HT_E

        for i in range(NT):
            xld = p.op(
                "g",
                lambda g, i=i: g.dma_start(
                    out=xin_b[:],
                    in_=ap["x_own"][i * 128:(i + 1) * 128, :]),
                waits=(bst["xin_free"],), inc="xinb", inc_k=16)
            p.op("v",
                 lambda v, i=i: v.tensor_scalar(
                     xn_fb[:], xin_b[:], m_col[:, i:i + 1],
                     rstd_col[:, i:i + 1], ALU.subtract, ALU.mult),
                 waits=(xld, bst["xnf_free"]))
            vev = p.op("v", lambda v: v.drain(), inc="v")
            bst["xin_free"] = vev
            for grp in range(4):
                b = grp % 2
                pev = p.op(
                    "t",
                    lambda t, grp=grp, b=b: [
                        t.matmul(ps_tr2[b][:, k * 128:(k + 1) * 128],
                                 xn_fb[:, (grp * 4 + k) * 128:
                                       (grp * 4 + k + 1) * 128],
                                 identb[:], start=True, stop=True)
                        for k in range(4)][-1],
                    waits=(vev, bst["tr_free"][b], ev_consts), inc="t")
                eev = p.op(
                    "v",
                    lambda v, grp=grp, b=b, i=i: v.tensor_copy(
                        xnT_b[:].rearrange("p (dt t) -> p dt t",
                                           dt=DT)[:, grp * 4:grp * 4 + 4,
                                                  i * 128:(i + 1) * 128],
                        ps_tr2[b][:, 0:512].rearrange(
                            "p (k c) -> p k c", k=4)) and v.drain(),
                    waits=(pev,), inc="v")
                bst["tr_free"][b] = eev
            bst["xnf_free"] = p.ev("t")
        xnT_b_ev = p.ev("v")

        def ffn_expert(wg_ap, wu_ap, wd_ap, nht, scale_cols):
            for ht in range(nht):
                b = ht % 2
                gld = p.op(
                    "g",
                    lambda g, wg_ap=wg_ap, ht=ht, b=b: g.dma_start(
                        out=wgc[b][:].rearrange("p (dt h) -> p dt h", dt=DT),
                        in_=wg_ap.rearrange("(dt p) h -> p dt h", p=128)
                            [:, :, ht * 128:(ht + 1) * 128]),
                    waits=(bst["wg_free"][b],), inc=f"wg{b}", inc_k=16)
                uld = p.op(
                    "g",
                    lambda g, wu_ap=wu_ap, ht=ht, b=b: g.dma_start(
                        out=wuc[b][:].rearrange("p (dt h) -> p dt h", dt=DT),
                        in_=wu_ap.rearrange("(dt p) h -> p dt h", p=128)
                            [:, :, ht * 128:(ht + 1) * 128]),
                    waits=(bst["wu_free"][b],), inc=f"wu{b}", inc_k=16)
                ev_gmm = p.op(
                    "t",
                    lambda t, b=b: [
                        t.matmul(ps_hg[:, tch * 512:(tch + 1) * 512],
                                 wgc[b][:, dt * 128:(dt + 1) * 128],
                                 xnT_b[:, (dt * NT + tch * 4) * 128:
                                       (dt * NT + tch * 4) * 128 + 512],
                                 start=(dt == 0), stop=(dt == DT - 1))
                        for dt in range(DT) for tch in range(2)][-1],
                    waits=(gld, xnT_b_ev, bst["hg_free"]), inc="t")
                ev_umm = p.op(
                    "t",
                    lambda t, b=b: [
                        t.matmul(ps_hu[:, tch * 512:(tch + 1) * 512],
                                 wuc[b][:, dt * 128:(dt + 1) * 128],
                                 xnT_b[:, (dt * NT + tch * 4) * 128:
                                       (dt * NT + tch * 4) * 128 + 512],
                                 start=(dt == 0), stop=(dt == DT - 1))
                        for dt in range(DT) for tch in range(2)][-1],
                    waits=(uld, bst["hu_free"]), inc="t")
                bst["wg_free"][b] = p.ev("t")
                bst["wu_free"][b] = p.ev("t")
                aev = p.op(
                    "a",
                    lambda a: [a.activation(silh[:, tch * 512:(tch + 1) * 512],
                                            ps_hg[:, tch * 512:(tch + 1) * 512],
                                            AF.Silu)
                               for tch in range(2)][-1] and a.drain(),
                    waits=(ev_gmm, bst["silh_free"]), inc="a")
                bst["hg_free"] = aev
                vev = p.op(
                    "v",
                    lambda v, ht=ht: [v.scalar_tensor_tensor(
                        hh[ht][:, tch * 512:(tch + 1) * 512],
                        ps_hu[:, tch * 512:(tch + 1) * 512], 1.0,
                        silh[:, tch * 512:(tch + 1) * 512],
                        ALU.mult, ALU.mult) for tch in range(2)][-1]
                    and v.drain(),
                    waits=(ev_umm, aev, bst["hh_free"][ht]), inc="v")
                bst["hu_free"] = vev
                bst["silh_free"] = vev
                hh_ev[ht] = vev
            kgroups = []
            s0 = 0
            while s0 < nht:
                kgroups.append((s0, min(s0 + 6, nht)))
                s0 += 6
            for (h0, h1) in kgroups:
                wd_ld = []
                for ht in range(h0, h1):
                    wd_ld.append(p.op(
                        "g",
                        lambda g, wd_ap=wd_ap, ht=ht, h0=h0: g.dma_start(
                            out=wdc[ht - h0][:],
                            in_=wd_ap[ht * 128:(ht + 1) * 128, :]),
                        waits=(bst["wd_free"][ht - h0],),
                        inc="wd", inc_k=16))
                for tt in range(NT):
                    for ddc in range(4):
                        slot = (tt * 4 + ddc) % 2
                        mmev = p.op(
                            "t",
                            lambda t, tt=tt, ddc=ddc, slot=slot,
                                   h0=h0, h1=h1: [
                                t.matmul(ps_dn[slot][:],
                                         hh[ht2][:, tt * 128:(tt + 1) * 128],
                                         wdc[ht2 - h0][:, ddc * 512:
                                                       (ddc + 1) * 512],
                                         start=(ht2 == h0),
                                         stop=(ht2 == h1 - 1))
                                for ht2 in range(h0, h1)][-1],
                            waits=tuple(hh_ev[h0:h1]) + tuple(wd_ld)
                                  + (bst["dn_free"][slot],),
                            inc="t")
                        if scale_cols is None:
                            vev = p.op(
                                "v",
                                lambda v, tt=tt, ddc=ddc, slot=slot:
                                v.tensor_add(
                                    acc[:, tt * D + ddc * 512:
                                        tt * D + (ddc + 1) * 512],
                                    acc[:, tt * D + ddc * 512:
                                        tt * D + (ddc + 1) * 512],
                                    ps_dn[slot][:]) and v.drain(),
                                waits=(mmev,), inc="v")
                        else:
                            vev = p.op(
                                "v",
                                lambda v, tt=tt, ddc=ddc, slot=slot,
                                       scale_cols=scale_cols:
                                v.scalar_tensor_tensor(
                                    acc[:, tt * D + ddc * 512:
                                        tt * D + (ddc + 1) * 512],
                                    ps_dn[slot][:], scale_cols(tt),
                                    acc[:, tt * D + ddc * 512:
                                        tt * D + (ddc + 1) * 512],
                                    ALU.mult, ALU.add) and v.drain(),
                                waits=(mmev,), inc="v")
                        bst["dn_free"][slot] = vev
                ge = p.ev("t")
                for k in range(6):
                    bst["wd_free"][k] = ge
            he = p.ev("t")
            for ht in range(nht):
                bst["hh_free"][ht] = he

        ffn_expert(ap["ws_gate"][:, 0:DE], ap["ws_up"][:, 0:DE],
                   ap["ws_down"][0:DE, :], HT_E, None)
        ffn_expert(ap["ws_gate"][:, DE:DS], ap["ws_up"][:, DE:DS],
                   ap["ws_down"][DE:DS, :], HT_E, None)

        r_free = None
        sc_free = None
        for tt in range(NT):
            mmev = p.op(
                "t",
                lambda t, tt=tt: [
                    t.matmul(ps_r[:, 0:E],
                             xnT_b[:, (dt * NT + tt) * 128:
                                   (dt * NT + tt) * 128 + 128],
                             wrb[:, dt * E:(dt + 1) * E],
                             start=(dt == 0), stop=(dt == DT - 1))
                    for dt in range(DT)][-1],
                waits=(ev_consts, xnT_b_ev, r_free, bst["dn_free"][0]),
                inc="t")
            mx = rcols[:, 0:1]
            sm = rcols[:, 1:2]
            rs = rcols[:, 2:3]
            m1 = rcols[:, 3:4]
            m2 = rcols[:, 4:5]
            vev = p.op("v", lambda v: v.reduce_max(
                    mx[:], ps_r[:, 0:E], axis=AX.X, negate=True) and v.drain(),
                waits=(mmev, sc_free), inc="v")
            aev = p.op("a", lambda a: a.activation(
                    scores[:], ps_r[:, 0:E], AF.Exp, bias=mx[:]) and a.drain(),
                waits=(vev,), inc="a")
            r_free = aev
            p.op("v", lambda v: v.reduce_sum(sm[:], scores[:], axis=AX.X),
                 waits=(aev,))
            p.op("v", lambda v: v.drain())
            p.op("v", lambda v: v.reciprocal(rs[:], sm[:]))
            p.op("v", lambda v: v.drain())
            p.op("v", lambda v: v.tensor_scalar(scores[:], scores[:], rs[:],
                                                None, ALU.mult))
            p.op("v", lambda v: v.drain())
            p.op("v", lambda v: v.reduce_max(m1[:], scores[:], axis=AX.X))
            p.op("v", lambda v: v.drain())
            p.op("v", lambda v: v.tensor_scalar(emask[:], scores[:], m1[:],
                                                None, ALU.is_ge))
            p.op("v", lambda v: v.drain())
            p.op("v", lambda v: v.scalar_tensor_tensor(
                    emask[:], emask[:], NEG, scores[:], ALU.mult, ALU.add))
            p.op("v", lambda v: v.drain())
            p.op("v", lambda v: v.reduce_max(m2[:], emask[:], axis=AX.X))
            p.op("v", lambda v: v.drain())
            p.op("v", lambda v: v.tensor_scalar(emask[:], scores[:], m2[:],
                                                None, ALU.is_ge))
            p.op("v", lambda v: v.drain())
            p.op("v", lambda v, tt=tt: v.tensor_mul(
                    combine[:, tt * E:(tt + 1) * E], emask[:], scores[:]))
            sc_free = p.op("v", lambda v: v.drain(), inc="v")
        bst["dn_free"][0] = r_free
        if "comb" in dbg:
            p.op("g", lambda sp: sp.dma_start(out=ap["dbg_comb"],
                                              in_=combine[:]),
                 waits=(p.ev("v"),), inc="dbg", inc_k=16)

        for e in range(E):
            ffn_expert(ap["we_gate"][e], ap["we_up"][e], ap["we_down"][e],
                       HT_E,
                       lambda tt, e=e: combine[:, tt * E + e: tt * E + e + 1])

        fin_free = None
        all_acc_done = p.ev("v")
        for i in range(NT):
            xld = p.op(
                "g",
                lambda g, i=i: g.dma_start(
                    out=xin_b[:],
                    in_=ap["x_own"][i * 128:(i + 1) * 128, :]),
                waits=(fin_free, bst["xin_free"], all_acc_done),
                inc="xinb", inc_k=16)
            p.op("v",
                 lambda v, i=i: v.tensor_scalar(
                     xn_fb[:], xin_b[:], m_col[:, i:i + 1],
                     rstd_col[:, i:i + 1], ALU.subtract, ALU.mult),
                 waits=(xld,))
            p.op("v", lambda v: v.drain())
            p.op("v", lambda v, i=i: v.tensor_add(
                    acc[:, i * D:(i + 1) * D], acc[:, i * D:(i + 1) * D],
                    xn_fb[:]))
            p.op("v", lambda v: v.drain())
            p.op("v", lambda v, i=i: v.tensor_add(
                    acc[:, i * D:(i + 1) * D], acc[:, i * D:(i + 1) * D],
                    xin_b[:]))
            vev = p.op("v", lambda v: v.drain(), inc="v")
            fin_free = vev
            bst["xin_free"] = vev
            p.op("s", lambda sp, i=i: sp.dma_start(
                    out=ap["y"][i * 128:(i + 1) * 128, :],
                    in_=acc[:, i * D:(i + 1) * D]),
                 waits=(vev,), inc="yst", inc_k=16)
        final_ev = p.ev("yst")
        p.op("s", lambda sp: None, waits=(final_ev,))
        p.flush()
        bs.close()

    return nc, ap



def make_inputs(full):
    x = np.asarray(full["L0_attn_x"], dtype=np.float32).reshape(S, D)
    w_router = np.asarray(full["L0_ffn_W_router"], dtype=np.float32)
    wrr = np.ascontiguousarray(
        w_router.reshape(DT, 128, E).transpose(1, 0, 2).reshape(128, DT * E))
    common = dict(
        x_prior=np.ascontiguousarray(x[0:PRIOR]),
        w_qkv=np.ascontiguousarray(full["L0_attn_w_qkv"], dtype=np.float32),
        w_swiglu=np.ascontiguousarray(full["L0_attn_w_swiglu"],
                                      dtype=np.float32),
        w_out=np.ascontiguousarray(full["L0_attn_w_out"], dtype=np.float32),
        w_router_r=wrr,
        we_gate=np.ascontiguousarray(full["L0_ffn_We_gate"], dtype=np.float32),
        we_up=np.ascontiguousarray(full["L0_ffn_We_up"], dtype=np.float32),
        we_down=np.ascontiguousarray(full["L0_ffn_We_down"],
                                     dtype=np.float32),
        ws_gate=np.ascontiguousarray(full["L0_ffn_Ws_gate"], dtype=np.float32),
        ws_up=np.ascontiguousarray(full["L0_ffn_Ws_up"], dtype=np.float32),
        ws_down=np.ascontiguousarray(full["L0_ffn_Ws_down"], dtype=np.float32),
        identb=np.eye(128, dtype=ml_dtypes.bfloat16),
        triu=np.triu(np.ones((128, 128))).astype(ml_dtypes.bfloat16),
        e0row=np.concatenate([np.ones((1, 128)), np.zeros((127, 128))],
                             axis=0).astype(np.float32),
        onesall=np.ones((128, 128), dtype=ml_dtypes.bfloat16),
        onescol_f=np.ones((128, 1), dtype=np.float32),
    )
    in_maps = []
    for c in range(NCORES):
        maskb = np.full((128, PRIOR // 128), NEG, dtype=np.float32)
        maskb[:, 0:NT * c] = 0.0
        m = dict(common)
        m["x_own"] = np.ascontiguousarray(x[c * TOK:(c + 1) * TOK])
        m["maskb"] = maskb
        in_maps.append(m)
    return in_maps


def kernel(**inputs):
    nc, _ = build_nc()
    in_maps = make_inputs(inputs)
    res = run_bass_kernel_spmd(nc, in_maps, core_ids=list(range(NCORES)))
    out = np.concatenate([res.results[c]["y"] for c in range(NCORES)], axis=0)
    return np.ascontiguousarray(out.reshape(1, S, D).astype(np.float32))


# version 10
# speedup vs baseline: 1.1511x; 1.1511x over previous
import contextlib
import numpy as np
import ml_dtypes
import concourse.bass as bass
import concourse.mybir as mybir
from concourse.bass_utils import run_bass_kernel_spmd

F32 = mybir.dt.float32
BF16 = mybir.dt.bfloat16
AF = mybir.ActivationFunctionType
ALU = mybir.AluOpType
AX = mybir.AxisListType

NCORES = 8
S = 8192
D = 2048
TOK = S // NCORES
NT = TOK // 128
PRIOR = S - TOK
DT = D // 128
JQ = 3 * D
DFF = 8192
F2 = 2 * DFF
DE = 1408
HT_E = DE // 128
DS = 2816
E = 8

LN_EPS = 1e-5
RMS_EPS = 1.1920929e-07
DEN_EPS = 1e-6
NEG = -1e30

P0_T = 4
A_T = 4
JCW = 256

ENGINES = ("t", "v", "a", "g", "s")


class Prog:
    def __init__(self, nc, es, block):
        self.nc = nc
        self.es = es
        self.block = block
        self.pend = {e: [] for e in ENGINES}
        self.sems = {}
        self.hwm = {}
        self.floor = ()

    def _sem(self, name):
        if name not in self.sems:
            h = self.es.enter_context(self.nc.semaphore(name))
            self.sems[name] = [h, 0]
        return self.sems[name]

    def op(self, eng, fn, waits=(), inc=None, inc_k=1):
        merged = {}
        for ev in tuple(self.floor) + tuple(waits):
            if ev is None:
                continue
            name, tgt = ev
            if tgt <= 0:
                continue
            if merged.get(name, -1) < tgt:
                merged[name] = tgt
        real = []
        for name, tgt in merged.items():
            key = (eng, name)
            if self.hwm.get(key, -1) >= tgt:
                continue
            self.hwm[key] = tgt
            real.append((name, tgt))
        out_ev = None
        if inc is not None:
            rec = self._sem(inc)
            rec[1] += inc_k
            out_ev = (inc, rec[1])
        sems = self.sems

        def thunk(engine, real=tuple(real), fn=fn, inc=inc, inc_k=inc_k):
            for name, tgt in real:
                engine.wait_ge(sems[name][0], tgt)
            instr = fn(engine)
            if inc is not None:
                assert instr is not None, "inc op returned no instruction"
                instr.then_inc(sems[inc][0], inc_k)

        self.pend[eng].append(thunk)
        return out_ev

    def ev(self, name):
        return (name, self._sem(name)[1])

    def barrier(self):
        self.floor = tuple(self.ev(n) for n in self.sems)

    def flush(self):
        names = {"t": "tensor", "v": "vector", "a": "scalar",
                 "g": "gpsimd", "s": "sync"}
        for e in ENGINES:
            stream = self.pend[e]
            if not stream:
                continue
            self.pend[e] = []

            def run(engine, stream=stream):
                for th in stream:
                    th(engine)
            getattr(self.block, names[e])(run)


DEBUG_KEYS = ()


def build_nc(debug_keys=DEBUG_KEYS, no_p0=False):
    nc = bass.Bass("TRN2", target_bir_lowering=False)
    dbg = set(debug_keys)

    ap = {}
    def din(name, shape, dtype=F32):
        ap[name] = nc.dram_tensor(name, shape, dtype, kind="ExternalInput").ap()
    def dout(name, shape, dtype=F32):
        ap[name] = nc.dram_tensor(name, shape, dtype, kind="ExternalOutput").ap()
    def dint(name, shape, dtype=BF16):
        ap[name] = nc.dram_tensor(name, shape, dtype).ap()

    din("x_own", [TOK, D])
    din("x_prior", [PRIOR, D])
    din("maskb", [128, PRIOR // 128])
    din("w_qkv", [JQ, D])
    din("w_swiglu", [F2, D])
    din("w_out", [D, DFF])
    din("w_router_r", [128, DT * E])
    din("we_gate", [E, D, DE])
    din("we_up", [E, D, DE])
    din("we_down", [E, DE, D])
    din("ws_gate", [D, DS])
    din("ws_up", [D, DS])
    din("ws_down", [DS, D])
    din("identb", [128, 128], BF16)
    din("triu", [128, 128], BF16)
    din("e0row", [128, 128], F32)
    din("onesall", [128, 128], BF16)
    din("onescol_f", [128, 1], F32)

    dout("y", [TOK, D])
    for k, shape in (("carry", [2, D]), ("q", [TOK, D]), ("k", [TOK, D]),
                     ("w0", [128, D]), ("yy", [TOK, D]),
                     ("comb", [128, NT * E]), ("accC", [128, NT * D]),
                     ("cols", [128, 32]), ("xnT", [128, DT * A_T * 128])):
        if k in dbg:
            dout("dbg_" + k, shape)

    dint("s_qkv_nat", [JQ, D])
    dint("s_qkvT", [JQ // JCW, 128, DT, JCW])
    dint("s_swg_nat", [F2, D])
    dint("s_swgT", [F2 // JCW, 128, DT, JCW])
    dint("s_wout_nat", [D, DFF])
    dint("s_woutT", [DFF, D])

    es = contextlib.ExitStack()
    with es:
        block = es.enter_context(nc.Block())
        p = Prog(nc, es, block)

        def sbuf(name, shape, dtype, stack=None, side=None):
            return (stack or es).enter_context(
                nc.sbuf_tensor(name, shape, dtype, side=side))

        def psum(name, shape, stack=None):
            return (stack or es).enter_context(nc.psum_tensor(name, shape, F32))

        m_col = sbuf("m_col", [128, NT], F32, side="left")
        rstd_col = sbuf("rstd_col", [128, NT], F32, side="left")
        combine = sbuf("combine", [128, NT * E], F32, side="left")
        carry_w = sbuf("carry_w", [128, D], F32, side="left")
        carry_wv = sbuf("carry_wv", [128, D], F32, side="left")
        identb = sbuf("identb_b", [128, 128], BF16, side="left")
        triu_b = sbuf("triu_b", [128, 128], BF16, side="left")
        e0row_b = sbuf("e0row_b", [128, 128], F32, side="left")
        onesall_b = sbuf("onesall_b", [128, 128], BF16, side="left")
        onescol_fb = sbuf("onescol_fb", [128, 1], F32, side="left")
        maskb = sbuf("maskb_b", [128, PRIOR // 128], F32, side="left")
        wrb = sbuf("wrb", [128, DT * E], BF16, side="left")
        cvals = sbuf("cvals", [128, 8], F32, side="left")
        for idx, val in enumerate((LN_EPS, RMS_EPS, DEN_EPS, -0.5, -1.0,
                                   1.0 / D)):
            p.op("v", lambda v, idx=idx, val=val:
                 v.memset(cvals[:, idx:idx + 1], val))
            nc.const_aps.register(F32, val, cvals[:, idx:idx + 1])
        ev_cvals = p.op("v", lambda v: v.drain(), inc="v")
        p.floor = (ev_cvals,)

        ys = contextlib.ExitStack()
        yT = sbuf("yT", [128, DT * NT * 128], BF16, ys, side="right")
        ss_stk = contextlib.ExitStack()
        strip_buf = [sbuf(f"strip{i}", [128, 2048], BF16, ss_stk, side="right")
                     for i in range(2)]

        for dst, src in ((identb, "identb"), (triu_b, "triu"),
                         (e0row_b, "e0row"), (onesall_b, "onesall"),
                         (onescol_fb, "onescol_f"), (maskb, "maskb"),
                         (wrb, "w_router_r")):
            p.op("g", lambda g, dst=dst, src=src:
                 g.dma_start(out=dst[:], in_=ap[src]), inc="gc", inc_k=16)
        ev_consts = p.ev("gc")

        cast_ev = {}
        cast_ev["qkv_kv"] = p.op(
            "g", lambda g: g.dma_start(out=ap["s_qkv_nat"][D:JQ, :],
                                       in_=ap["w_qkv"][D:JQ, :]),
            inc="cast_qkv_kv", inc_k=16)
        cast_ev["qkv_q"] = p.op(
            "g", lambda g: g.dma_start(out=ap["s_qkv_nat"][0:D, :],
                                       in_=ap["w_qkv"][0:D, :]),
            inc="cast_qkv_q", inc_k=16)
        for tag, src, dst in (("swg", "w_swiglu", "s_swg_nat"),
                              ("wout", "w_out", "s_wout_nat")):
            cast_ev[tag] = p.op(
                "g", lambda g, src=src, dst=dst:
                g.dma_start(out=ap[dst], in_=ap[src]),
                inc="cast_" + tag, inc_k=16)

        jobs = []
        for hh in (1, 2):
            for ct in range(DT):
                jobs.append(("s_qkv_nat", "s_qkvT", ct, hh * 2048, 2048,
                             "qkv_kv"))
        for ct in range(DT):
            jobs.append(("s_qkv_nat", "s_qkvT", ct, 0, 2048, "qkv_q"))
        for ct in range(DT):
            for hh in range(8):
                jobs.append(("s_swg_nat", "s_swgT", ct, hh * 2048, 2048, "swg"))
        for ct in range(DFF // 128):
            jobs.append(("s_wout_nat", "s_woutT", ct, 0, 2048, "wout"))
        store_ev = [None, None]
        t_ready = {}
        for i, (nat, dstT, ct, r0, rn, tag) in enumerate(jobs):
            b = strip_buf[i % 2]
            qeng = "a" if (tag == "qkv_kv" and i % 2 == 1) else "s"
            tev = p.op(
                qeng,
                lambda sp, nat=nat, ct=ct, r0=r0, rn=rn, b=b: sp.dma_start(
                    out=b[:, 0:rn],
                    in_=ap[nat][r0:r0 + rn, ct * 128:(ct + 1) * 128],
                    transpose=True),
                waits=(cast_ev[tag], store_ev[i % 2]),
                inc=f"strip{i % 2}", inc_k=16)
            if tag == "wout":
                sev = p.op(
                    qeng,
                    lambda sp, dstT=dstT, ct=ct, r0=r0, rn=rn, b=b:
                    sp.dma_start(
                        out=ap[dstT][ct * 128:(ct + 1) * 128, r0:r0 + rn],
                        in_=b[:, 0:rn]),
                    waits=(tev,), inc=f"strip{i % 2}", inc_k=16)
            else:
                for jl in range(rn // JCW):
                    jc = (r0 + jl * JCW) // JCW
                    sev = p.op(
                        qeng,
                        lambda sp, dstT=dstT, ct=ct, jc=jc, jl=jl, b=b:
                        sp.dma_start(
                            out=ap[dstT][jc, :, ct, :],
                            in_=b[:, jl * JCW:(jl + 1) * JCW]),
                        waits=(tev,), inc=f"strip{i % 2}", inc_k=16)
            store_ev[i % 2] = sev
            t_ready[tag] = (p.ev("strip0"), p.ev("strip1"))
        kvT_ready = t_ready["qkv_kv"]
        qT_ready = t_ready["qkv_q"]
        swgT_ready = t_ready["swg"]
        woutT_ready = t_ready["wout"]
        p.flush()

        shs = contextlib.ExitStack()
        xin = sbuf("xin", [128, D], F32, shs, side="right")
        xn3 = [sbuf(f"xn3_{i}", [128, D], BF16, shs, side="right")
               for i in range(4)]
        xnT_a = sbuf("xnT_a", [128, DT * P0_T * 128], BF16, shs, side="right")
        kscr = sbuf("kscr", [128, P0_T * D], F32, shs, side="right")
        vscr = sbuf("vscr", [128, P0_T * D], BF16, shs, side="right")
        wslab = [sbuf(f"wslab{i}", [128, DT * JCW], BF16, shs, side="right")
                 for i in range(2)]
        w_bf = sbuf("w_bf", [128, D], BF16, shs, side="right")
        wv_bf = sbuf("wv_bf", [128, D], BF16, shs, side="right")
        lncols = sbuf("lncols", [128, 8], F32, shs, side="right")
        sq_scr = sbuf("sq_scr", [128, D], F32, shs, side="right")
        srms8 = sbuf("srms8", [128, NT], F32, shs, side="right")
        ps_kv = [psum(f"ps_kv{i}", [128, 512], shs) for i in range(8)]
        p0s = contextlib.ExitStack()
        acc_w = sbuf("acc_w", [128, D], F32, p0s, side="right")
        acc_wv = sbuf("acc_wv", [128, D], F32, p0s, side="right")
        w_f = sq_scr

        p.op("v", lambda v: v.memset(acc_w[:], 0.0))
        p.op("v", lambda v: v.memset(acc_wv[:], 0.0))
        p.op("v", lambda v: v.memset(carry_w[:], 0.0))
        p.op("v", lambda v: v.memset(carry_wv[:], 0.0))
        p.op("v", lambda v: v.drain(), inc="v")

        st = {
            "xin_free": None,
            "xnf_free": [None, None, None, None],
            "slab_free": [None, None],
            "bank_free": [None] * 8,
            "wbf_free": None, "wvbf_free": None, "wff_free": None,
            "yf_free": None,
        }

        def ln_tile(xin, xev, xn_out, xn_w_wait, mm, rr, sscol):
            sums = lncols[:, 0:1]
            ss = lncols[:, 1:2]
            var = lncols[:, 2:3]
            t1 = lncols[:, 3:4]
            lnv = lncols[:, 4:5]
            p.op("v", lambda v: v.reduce_sum(sums[:], xin[:], axis=AX.X),
                 waits=(xev,))
            p.op("v", lambda v: v.drain())
            p.op("a", lambda a: a.activation(sq_scr[:], xin[:], AF.Square,
                                             accum_out=ss[:]),
                 waits=(xev, st["wff_free"]))
            aev = p.op("a", lambda a: a.drain(), inc="a")
            p.op("v", lambda v: v.tensor_scalar(mm[:], sums[:], 1.0 / D, None,
                                                ALU.mult))
            p.op("v", lambda v: v.drain())
            p.op("v", lambda v: v.tensor_mul(t1[:], mm[:], mm[:]), waits=(aev,))
            p.op("v", lambda v: v.drain())
            p.op("v", lambda v: v.scalar_tensor_tensor(
                var[:], ss[:], 1.0 / D, t1[:], ALU.mult, ALU.subtract))
            vev = p.op("v", lambda v: v.drain(), inc="v")
            p.op("a", lambda a: a.activation(lnv[:], var[:], AF.Ln,
                                             bias=LN_EPS), waits=(vev,))
            p.op("a", lambda a: a.drain())
            p.op("a", lambda a: a.activation(rr[:], lnv[:], AF.Exp, scale=-0.5))
            aev2 = p.op("a", lambda a: a.drain(), inc="a")
            p.op("v", lambda v: v.tensor_mul(t1[:], var[:], rr[:]),
                 waits=(aev2,))
            p.op("v", lambda v: v.drain())
            p.op("v", lambda v: v.tensor_mul(t1[:], t1[:], rr[:]))
            vev2 = p.op("v", lambda v: v.drain(), inc="v")
            p.op("a", lambda a: a.activation(lnv[:], t1[:], AF.Ln,
                                             bias=RMS_EPS), waits=(vev2,))
            p.op("a", lambda a: a.drain())
            p.op("a", lambda a: a.activation(sscol[:], lnv[:], AF.Exp,
                                             scale=-0.5))
            aev3 = p.op("a", lambda a: a.drain(), inc="a")
            p.op("v", lambda v: v.tensor_scalar(xn_out[:], xin[:], mm[:],
                                                rr[:], ALU.subtract, ALU.mult),
                 waits=(aev3, xn_w_wait))
            return p.op("v", lambda v: v.drain(), inc="v")

        def transpose_tile(src_bf, ev_src, dstT, tt, nt_cols, banks=(4, 5)):
            last = None
            for grp in range(4):
                b = banks[grp % 2]
                pev = p.op(
                    "t",
                    lambda t, grp=grp, b=b, src_bf=src_bf: [
                        t.matmul(ps_kv[b][:, k * 128:(k + 1) * 128],
                                 src_bf[:, (grp * 4 + k) * 128:
                                        (grp * 4 + k + 1) * 128],
                                 identb[:], start=True, stop=True)
                        for k in range(4)][-1],
                    waits=(ev_src, st["bank_free"][b], ev_consts),
                    inc="t")
                vev = p.op(
                    "v",
                    lambda v, grp=grp, b=b, tt=tt, dstT=dstT,
                           nt_cols=nt_cols: v.tensor_copy(
                        dstT[:, 0:DT * nt_cols * 128]
                        .rearrange("p (dt t) -> p dt t",
                                   dt=DT)[:, grp * 4:grp * 4 + 4,
                                          tt * 128:(tt + 1) * 128],
                        ps_kv[b][:, 0:512].rearrange(
                            "p (k c) -> p k c", k=4)) and v.drain(),
                    waits=(pev,), inc="v")
                st["bank_free"][b] = vev
                last = vev
            return last

        pcols = sbuf("pcols", [128, 8], F32, side="left")

        def prelude_tile(own, base_tok, gt, i):
            xsrc = "x_own" if own else "x_prior"
            xld = p.op(
                "g",
                lambda g, i=i, base_tok=base_tok, xsrc=xsrc: g.dma_start(
                    out=xin[:],
                    in_=ap[xsrc][base_tok + i * 128:
                                 base_tok + (i + 1) * 128, :]),
                waits=(st["xin_free"],), inc="xin0", inc_k=16)
            ti_own = (base_tok // 128) + i if own else i
            mm = m_col[:, ti_own:ti_own + 1] if own else lncols[:, 6:7]
            rr = (rstd_col[:, ti_own:ti_own + 1] if own
                  else lncols[:, 7:8])
            sc = srms8[:, i:i + 1]
            buf = xn3[i % 4]
            ev_xn = ln_tile(xin, xld, buf, st["xnf_free"][i % 4],
                            mm, rr, sc)
            st["xin_free"] = ev_xn
            return (ev_xn, buf, i % 4)

        def prelude_av(own, base_tok, gt):
            return [prelude_tile(own, base_tok, gt, i) for i in range(gt)]

        def prelude_t(pre, gt):
            tr_evs = []
            for i, (ev_xn, buf, slot3) in enumerate(pre):
                ev_tr = transpose_tile(buf, ev_xn, xnT_a, i, gt, (4, 5))
                st["xnf_free"][slot3] = p.ev("t")
                tr_evs.append(ev_tr)
            return tr_evs

        def qkv_mms(own, gt, v_gates, tr_evs, nxt_spec=None):
            nq = D // JCW
            jcs = (list(range(0, 3 * nq)) if own
                   else list(range(nq, 3 * nq)))
            ev_q = [None] * gt
            ev_k = [None] * gt
            ev_v = [None] * gt
            hook_res = [None]
            if nxt_spec is not None:
                n_own, n_base, n_gt = nxt_spec
                hook_res[0] = []
                hook_slots = {4 + 3 * i: i for i in range(n_gt)}
            for slot, jc in enumerate(jcs):
                if nxt_spec is not None and slot in hook_slots:
                    hook_res[0].append(
                        prelude_tile(n_own, n_base, n_gt,
                                     hook_slots[slot]))
                sl = wslab[slot % 2]
                ready = qT_ready if jc < nq else kvT_ready
                sev = p.op(
                    "g",
                    lambda g, jc=jc, sl=sl: g.dma_start(
                        out=sl[:].rearrange("p (dt j) -> p dt j", dt=DT),
                        in_=ap["s_qkvT"][jc]),
                    waits=(st["slab_free"][slot % 2],) + ready,
                    inc=f"slab{slot % 2}", inc_k=16)
                for tt in range(gt):
                    mmev = p.op(
                        "t",
                        lambda t, tt=tt, sl=sl, gt=gt: [
                            t.matmul(ps_kv[tt][:, 0:JCW],
                                     xnT_a[:, (dt * gt + tt) * 128:
                                           (dt * gt + tt) * 128 + 128],
                                     sl[:, dt * JCW:(dt + 1) * JCW],
                                     start=(dt == 0), stop=(dt == DT - 1))
                            for dt in range(DT)][-1],
                        waits=(sev, tr_evs[tt], st["bank_free"][tt]),
                        inc="t")
                    if jc < nq:
                        dstbuf, col, dd = aonly["qscr"], jc * JCW, ev_q
                    elif jc < 2 * nq:
                        dstbuf, col, dd = kscr, (jc - nq) * JCW, ev_k
                    else:
                        dstbuf, col, dd = vscr, (jc - 2 * nq) * JCW, ev_v
                    vg = v_gates[tt] if (dstbuf is vscr and v_gates) else None
                    if tt % 2 == 0:
                        p.op("a",
                             lambda a, tt=tt, dstbuf=dstbuf, col=col:
                             a.activation(
                                 dstbuf[:, tt * D + col: tt * D + col + JCW],
                                 ps_kv[tt][:, 0:JCW], AF.Copy,
                                 scale=srms8[:, tt:tt + 1]),
                             waits=(mmev, vg))
                        aev = p.op("a", lambda a: a.drain(), inc="a")
                    else:
                        p.op("v",
                             lambda v, tt=tt, dstbuf=dstbuf, col=col:
                             v.tensor_scalar(
                                 dstbuf[:, tt * D + col: tt * D + col + JCW],
                                 ps_kv[tt][:, 0:JCW],
                                 srms8[:, tt:tt + 1], None, ALU.mult),
                             waits=(mmev, vg))
                        aev = p.op("v", lambda v: v.drain(), inc="v")
                    st["bank_free"][tt] = aev
                    dd[tt] = aev
                st["slab_free"][slot % 2] = p.ev("t")
            return ev_q, ev_k, ev_v, hook_res[0]

        def p0_posts(hb, ev_k, ev_v):
            ss = pcols[:, 0:1]
            lnv = pcols[:, 1:2]
            invk = pcols[:, 2:3]
            gates = [None] * P0_T
            for i in range(P0_T):
                gtile = hb * P0_T + i
                p.op("a", lambda a, i=i: a.activation(
                        sq_scr[:], kscr[:, i * D:(i + 1) * D], AF.Square,
                        accum_out=ss[:]), waits=(ev_k[i], st["wff_free"]))
                p.op("a", lambda a: a.drain())
                p.op("a", lambda a: a.activation(lnv[:], ss[:], AF.Ln,
                                                 scale=1.0 / D, bias=RMS_EPS))
                p.op("a", lambda a: a.drain())
                p.op("a", lambda a: a.activation(invk[:], lnv[:], AF.Exp,
                                                 scale=-0.5))
                p.op("a", lambda a: a.drain())
                p.op("a", lambda a, i=i, gtile=gtile: a.activation(
                        w_f[:], kscr[:, i * D:(i + 1) * D], AF.Exp,
                        bias=maskb[:, gtile:gtile + 1], scale=invk[:]),
                     waits=(ev_consts,))
                aev = p.op("a", lambda a: a.drain(), inc="a")
                p.op("v", lambda v: v.tensor_add(acc_w[:], acc_w[:], w_f[:]),
                     waits=(aev,))
                p.op("v", lambda v: v.drain())
                p.op("v", lambda v, i=i: v.tensor_mul(
                        w_f[:], w_f[:], vscr[:, i * D:(i + 1) * D]),
                     waits=(ev_v[i],))
                gates[i] = p.op("v", lambda v: v.drain(), inc="v")
                p.op("v", lambda v: v.tensor_add(acc_wv[:], acc_wv[:],
                                                 w_f[:]))
                st["wff_free"] = p.op("v", lambda v: v.drain(), inc="v")
            return gates

        def a_posts(qb, ev_q, ev_k, ev_v):
            ss = pcols[:, 0:1]
            lnv = pcols[:, 1:2]
            invk = pcols[:, 2:3]
            invq2 = pcols[:, 3:3 + A_T]
            gates = [None] * A_T
            if "q" in dbg:
                for i in range(A_T):
                    gi = qb * A_T + i
                    p.op("g", lambda sp, i=i, gi=gi: sp.dma_start(
                            out=ap["dbg_q"][gi * 128:(gi + 1) * 128, :],
                            in_=qscr[:, i * D:(i + 1) * D]),
                         waits=(ev_q[i],), inc="dbg", inc_k=16)
            if "k" in dbg:
                for i in range(A_T):
                    gi = qb * A_T + i
                    p.op("g", lambda sp, i=i, gi=gi: sp.dma_start(
                            out=ap["dbg_k"][gi * 128:(gi + 1) * 128, :],
                            in_=kscr[:, i * D:(i + 1) * D]),
                         waits=(ev_k[i],), inc="dbg", inc_k=16)
            if "xnT" in dbg and qb == 0:
                p.op("g", lambda sp: sp.dma_start(
                        out=ap["dbg_xnT"], in_=xnT_a[:, 0:DT * A_T * 128]),
                     waits=(p.ev("v"),), inc="dbg", inc_k=16)
            if dbg:
                p.barrier()
            for i in range(A_T):
                gi = qb * A_T + i
                p.op("a", lambda a, i=i: a.activation(
                        sq_scr[:], qscr[:, i * D:(i + 1) * D], AF.Square,
                        accum_out=ss[:]), waits=(ev_q[i],))
                p.op("a", lambda a: a.drain())
                p.op("a", lambda a: a.activation(lnv[:], ss[:], AF.Ln,
                                                 scale=1.0 / D, bias=RMS_EPS))
                p.op("a", lambda a: a.drain())
                p.op("a", lambda a, i=i: a.activation(
                        invq2[:, i:i + 1], lnv[:], AF.Exp, scale=-0.5))
                p.op("a", lambda a: a.drain())
                p.op("a", lambda a, i=i: a.activation(
                        sq_scr[:], kscr[:, i * D:(i + 1) * D], AF.Square,
                        accum_out=ss[:]), waits=(ev_k[i],))
                p.op("a", lambda a: a.drain())
                p.op("a", lambda a: a.activation(lnv[:], ss[:], AF.Ln,
                                                 scale=1.0 / D, bias=RMS_EPS))
                p.op("a", lambda a: a.drain())
                p.op("a", lambda a: a.activation(invk[:], lnv[:], AF.Exp,
                                                 scale=-0.5))
                p.op("a", lambda a: a.drain())
                p.op("a", lambda a, i=i: a.activation(
                        w_bf[:], kscr[:, i * D:(i + 1) * D], AF.Exp,
                        scale=invk[:]), waits=(st["wbf_free"],))
                aev = p.op("a", lambda a: a.drain(), inc="a")
                if "w0" in dbg and gi == 0:
                    p.op("g", lambda sp: sp.dma_start(out=ap["dbg_w0"],
                                                      in_=w_bf[:]),
                         waits=(aev,), inc="dbg", inc_k=16)
                    p.barrier()
                p.op("v", lambda v, i=i: v.tensor_mul(
                        wv_bf[:], w_bf[:], vscr[:, i * D:(i + 1) * D]),
                     waits=(aev, ev_v[i], st["wvbf_free"]))
                vev_wv = p.op("v", lambda v: v.drain(), inc="v")
                gates[i] = vev_wv
                rscr_free = [None, None]
                for ch in range(4):
                    bA = 4 + (ch % 2) * 2
                    bB = 5 + (ch % 2) * 2
                    rs = r_scr[:, 0:512]
                    mmA = p.op(
                        "t",
                        lambda t, ch=ch, bA=bA: (
                            t.matmul(ps_kv[bA][:], triu_b[:],
                                     w_bf[:, ch * 512:(ch + 1) * 512],
                                     start=True, stop=False),
                            t.matmul(ps_kv[bA][:], e0row_b[:],
                                     carry_w[:, ch * 512:(ch + 1) * 512],
                                     start=False, stop=True))[-1],
                        waits=(aev, st["bank_free"][bA], p.ev("v")),
                        inc="t")
                    mmB = p.op(
                        "t",
                        lambda t, ch=ch, bB=bB: (
                            t.matmul(ps_kv[bB][:], triu_b[:],
                                     wv_bf[:, ch * 512:(ch + 1) * 512],
                                     start=True, stop=False),
                            t.matmul(ps_kv[bB][:], e0row_b[:],
                                     carry_wv[:, ch * 512:(ch + 1) * 512],
                                     start=False, stop=True))[-1],
                        waits=(vev_wv, st["bank_free"][bB]),
                        inc="t")
                    p.op("a", lambda a, bA=bA, rs=rs: a.activation(
                            rs[:], ps_kv[bA][:], AF.Ln, bias=DEN_EPS),
                         waits=(mmA, rscr_free[0]))
                    p.op("a", lambda a: a.drain())
                    p.op("a", lambda a, rs=rs: a.activation(
                            rs[:], rs[:], AF.Exp, scale=-1.0))
                    aevr = p.op("a", lambda a: a.drain(), inc="a")
                    st["bank_free"][bA] = aevr
                    vev = p.op(
                        "v",
                        lambda v, ch=ch, bB=bB, rs=rs:
                        v.tensor_tensor(
                            y_f[:, ch * 512:(ch + 1) * 512],
                            ps_kv[bB][:], rs[:], ALU.mult) and v.drain(),
                        waits=(mmB, aevr, st["yf_free"]), inc="v")
                    st["bank_free"][bB] = vev
                    rscr_free[0] = vev
                for src, dstrow, wsrc in ((w_bf, carry_w, aev),
                                          (wv_bf, carry_wv, vev_wv)):
                    for ch in range(4):
                        bank = 4 + ((ch + (0 if src is w_bf else 1)) % 4)
                        mmev = p.op(
                            "t",
                            lambda t, src=src, ch=ch, bank=bank: t.matmul(
                                ps_kv[bank][:], onesall_b[:],
                                src[:, ch * 512:(ch + 1) * 512],
                                start=True, stop=True),
                            waits=(st["bank_free"][bank], wsrc, ev_consts),
                            inc="t")
                        vev = p.op(
                            "v",
                            lambda v, dstrow=dstrow, ch=ch, bank=bank:
                            v.tensor_add(
                                dstrow[0:1, ch * 512:(ch + 1) * 512],
                                dstrow[0:1, ch * 512:(ch + 1) * 512],
                                ps_kv[bank][0:1, :]) and v.drain(),
                            waits=(mmev,), inc="v")
                        st["bank_free"][bank] = vev
                p.op("v", lambda v: v.drain(), inc="v")
                st["wbf_free"] = p.ev("t")
                st["wvbf_free"] = p.ev("t")
                aev = p.op(
                    "a",
                    lambda a, i=i: a.activation(
                        sigq_t[:], qscr[:, i * D:(i + 1) * D], AF.Sigmoid,
                        scale=invq2[:, i:i + 1]) and a.drain(),
                    inc="a")
                vev = p.op(
                    "v",
                    lambda v: v.tensor_mul(y_f[:], y_f[:],
                                           sigq_t[:]) and v.drain(),
                    waits=(aev,), inc="v")
                transpose_tile(y_f, vev, yT, gi, NT, (2, 3))
                st["yf_free"] = p.ev("t")
                if "yy" in dbg:
                    p.op("g", lambda sp, gi=gi: sp.dma_start(
                            out=ap["dbg_yy"][gi * 128:(gi + 1) * 128, :],
                            in_=y_f[:]),
                         waits=(vev,), inc="dbg", inc_k=16)
                    p.barrier()
            return gates

        aos = contextlib.ExitStack()
        aonly = {}

        nP0 = 0 if no_p0 else PRIOR // (P0_T * 128)
        post_gate = None
        if nP0:
            tr_evs = prelude_t(prelude_av(False, 0, P0_T), P0_T)
        for hb in range(nP0):
            if hb + 1 < nP0:
                spec = (False, (hb + 1) * P0_T * 128, P0_T)
                ngt = P0_T
            else:
                spec = (True, 0, A_T)
                ngt = A_T
            _, ev_k, ev_v, nxt = qkv_mms(False, P0_T, post_gate, tr_evs,
                                         spec)
            tr_evs = prelude_t(nxt, ngt)
            post_gate = p0_posts(hb, ev_k, ev_v)
        if nP0 == 0:
            tr_evs = prelude_t(prelude_av(True, 0, A_T), A_T)

        accred_ev = p.ev("v")
        for src, dst in ((acc_w, carry_w), (acc_wv, carry_wv)):
            for ch in range(4):
                bank = ch
                mmev = p.op(
                    "t",
                    lambda t, src=src, ch=ch, bank=bank: t.matmul(
                        ps_kv[bank][0:1, :], onescol_fb[:],
                        src[:, ch * 512:(ch + 1) * 512],
                        start=True, stop=True),
                    waits=(accred_ev, st["bank_free"][bank], ev_consts),
                    inc="t")
                vev = p.op(
                    "v",
                    lambda v, dst=dst, ch=ch, bank=bank: v.tensor_copy(
                        dst[0:1, ch * 512:(ch + 1) * 512],
                        ps_kv[bank][0:1, :]) and v.drain(),
                    waits=(mmev,), inc="v")
                st["bank_free"][bank] = vev
        carry_ev = p.ev("v")
        p.flush()
        p.barrier()
        p0s.close()
        aonly["qscr"] = sbuf("qscr", [128, A_T * D], BF16, aos, side="right")
        qscr = aonly["qscr"]
        y_f = sbuf("y_f", [128, D], BF16, aos, side="right")
        sigq_t = sbuf("sigq_t", [128, D], BF16, aos, side="right")
        r_scr = sbuf("r_scr", [128, 512], F32, aos, side="right")
        if "carry" in dbg:
            p.op("g", lambda sp: sp.dma_start(out=ap["dbg_carry"][0:1, :],
                                              in_=carry_w[0:1, :]),
                 waits=(carry_ev,), inc="dbg", inc_k=16)
            p.op("g", lambda sp: sp.dma_start(out=ap["dbg_carry"][1:2, :],
                                              in_=carry_wv[0:1, :]),
                 waits=(carry_ev,), inc="dbg", inc_k=16)
            p.barrier()

        nA = TOK // (A_T * 128)
        gateA = None
        for qb in range(nA):
            spec = ((True, (qb + 1) * A_T * 128, A_T)
                    if qb + 1 < nA else None)
            ev_q, ev_k, ev_v, nxt = qkv_mms(True, A_T, gateA, tr_evs, spec)
            if nxt is not None:
                tr_evs = prelude_t(nxt, A_T)
            gateA = a_posts(qb, ev_q, ev_k, ev_v)
        yT_ev = p.ev("v")
        p.flush()
        p.barrier()
        aos.close()
        shs.close()
        ss_stk.close()

        acc = sbuf("acc", [128, NT * D], F32, side="left")
        cs = contextlib.ExitStack()
        uslab = [sbuf(f"uslab{i}", [128, DT * 256], BF16, cs, side="right")
                 for i in range(2)]
        gslab = [sbuf(f"gslab{i}", [128, DT * 256], BF16, cs, side="right")
                 for i in range(2)]
        zbuf = [sbuf(f"zbuf{i}", [128, TOK], BF16, cs, side="right")
                for i in range(8)]
        silg = sbuf("silg", [128, TOK], F32, cs, side="right")
        wo_ch = [sbuf(f"wo_ch{i}", [128, 512], BF16, cs, side="right")
                 for i in range(16)]
        ps_u = psum("ps_u", [128, 1024], cs)
        ps_g = psum("ps_g", [128, 1024], cs)
        ps_wo = [psum(f"ps_wo{i}", [128, 512], cs) for i in range(2)]

        cst = {"uslab_free": [None, None], "gslab_free": [None, None],
               "wo_free": [None] * 16, "zfree": [None] * 8,
               "psu_free": None, "psg_free": None, "silg_free": None,
               "pswo_free": [None, None], "uld": None, "gld": None}
        z_ev = [None] * 8
        first_group = True
        for fg in range(8):
            for fi in range(8):
                f = fg * 8 + fi
                par = (f // 2) % 2
                if fi % 2 == 0:
                    cst["uld"] = p.op(
                        "g",
                        lambda g, f=f, par=par: g.dma_start(
                            out=uslab[par][:].rearrange(
                                "p (dt j) -> p dt j", dt=DT),
                            in_=ap["s_swgT"][f // 2]),
                        waits=swgT_ready + (cst["uslab_free"][par],),
                        inc=f"usl{par}", inc_k=16)
                    cst["gld"] = p.op(
                        "g",
                        lambda g, f=f, par=par: g.dma_start(
                            out=gslab[par][:].rearrange(
                                "p (dt j) -> p dt j", dt=DT),
                            in_=ap["s_swgT"][DFF // JCW + f // 2]),
                        waits=(cst["gslab_free"][par],),
                        inc=f"gsl{par}", inc_k=16)
                off = (f % 2) * 128
                ev_u = p.op(
                    "t",
                    lambda t, par=par, off=off: [
                        t.matmul(ps_u[:, tch * 512:(tch + 1) * 512],
                                 uslab[par][:, dt * 256 + off:
                                            dt * 256 + off + 128],
                                 yT[:, (dt * NT + tch * 4) * 128:
                                    (dt * NT + tch * 4) * 128 + 512],
                                 start=(dt == 0), stop=(dt == DT - 1))
                        for tch in range(2) for dt in range(DT)][-1],
                    waits=(cst["uld"], yT_ev, cst["psu_free"]),
                    inc="t")
                ev_g = p.op(
                    "t",
                    lambda t, par=par, off=off: [
                        t.matmul(ps_g[:, tch * 512:(tch + 1) * 512],
                                 gslab[par][:, dt * 256 + off:
                                            dt * 256 + off + 128],
                                 yT[:, (dt * NT + tch * 4) * 128:
                                    (dt * NT + tch * 4) * 128 + 512],
                                 start=(dt == 0), stop=(dt == DT - 1))
                        for tch in range(2) for dt in range(DT)][-1],
                    waits=(cst["gld"], cst["psg_free"]),
                    inc="t")
                if fi % 2 == 1:
                    cst["uslab_free"][par] = p.ev("t")
                    cst["gslab_free"][par] = p.ev("t")
                aev = p.op(
                    "a",
                    lambda a: [a.activation(silg[:, tch * 512:(tch + 1) * 512],
                                            ps_g[:, tch * 512:(tch + 1) * 512],
                                            AF.Silu)
                               for tch in range(2)][-1] and a.drain(),
                    waits=(ev_g, cst["silg_free"]), inc="a")
                cst["psg_free"] = aev
                vev = p.op(
                    "v",
                    lambda v, fi=fi: [v.scalar_tensor_tensor(
                        zbuf[fi][:, tch * 512:(tch + 1) * 512],
                        ps_u[:, tch * 512:(tch + 1) * 512], 1.0,
                        silg[:, tch * 512:(tch + 1) * 512],
                        ALU.mult, ALU.mult) for tch in range(2)][-1]
                    and v.drain(),
                    waits=(ev_u, aev, cst["zfree"][fi]), inc="v")
                cst["psu_free"] = vev
                cst["silg_free"] = vev
                z_ev[fi] = vev
            for ddc in range(4):
                half = (ddc % 2) * 8
                wo_ld = []
                for fi in range(8):
                    f = fg * 8 + fi
                    wo_ld.append(p.op(
                        "s",
                        lambda sp, f=f, fi=fi, ddc=ddc, half=half:
                        sp.dma_start(
                            out=wo_ch[half + fi][:],
                            in_=ap["s_woutT"][f * 128:(f + 1) * 128,
                                              ddc * 512:(ddc + 1) * 512]),
                        waits=woutT_ready + (cst["wo_free"][half + fi],),
                        inc=f"wo{ddc % 2}", inc_k=16))
                for tt in range(NT):
                    slot = (tt * 4 + ddc) % 2
                    mmev = p.op(
                        "t",
                        lambda t, tt=tt, ddc=ddc, slot=slot, half=half: [
                            t.matmul(ps_wo[slot][:],
                                     zbuf[fi2][:, tt * 128:(tt + 1) * 128],
                                     wo_ch[half + fi2][:],
                                     start=(fi2 == 0), stop=(fi2 == 7))
                            for fi2 in range(8)][-1],
                        waits=tuple(z_ev) + tuple(wo_ld)
                              + (cst["pswo_free"][slot],),
                        inc="t")
                    if first_group:
                        vev = p.op(
                            "v",
                            lambda v, tt=tt, ddc=ddc, slot=slot:
                            v.tensor_copy(
                                acc[:, tt * D + ddc * 512:
                                    tt * D + (ddc + 1) * 512],
                                ps_wo[slot][:]) and v.drain(),
                            waits=(mmev,), inc="v")
                    else:
                        vev = p.op(
                            "v",
                            lambda v, tt=tt, ddc=ddc, slot=slot:
                            v.tensor_add(
                                acc[:, tt * D + ddc * 512:
                                    tt * D + (ddc + 1) * 512],
                                acc[:, tt * D + ddc * 512:
                                    tt * D + (ddc + 1) * 512],
                                ps_wo[slot][:]) and v.drain(),
                            waits=(mmev,), inc="v")
                    cst["pswo_free"][slot] = vev
                woe = p.ev("t")
                for fi in range(8):
                    cst["wo_free"][half + fi] = woe
            first_group = False
            ze = p.ev("t")
            for fi in range(8):
                cst["zfree"][fi] = ze
        if "accC" in dbg:
            p.op("g", lambda sp: sp.dma_start(out=ap["dbg_accC"], in_=acc[:]),
                 waits=(p.ev("v"),), inc="dbg", inc_k=16)
        p.flush()
        p.barrier()
        cs.close()
        ys.close()

        bs = contextlib.ExitStack()
        xnT_b = sbuf("xnT_bb", [128, DT * NT * 128], BF16, bs, side="right")
        xin_b = sbuf("xin_b", [128, D], F32, bs, side="right")
        xn_fb = sbuf("xn_fb", [128, D], BF16, bs, side="right")
        hh = [sbuf(f"hh{i}", [128, TOK], BF16, bs, side="right")
              for i in range(HT_E)]
        wgc = [sbuf(f"wgc{i}", [128, DT * 128], BF16, bs, side="right")
               for i in range(2)]
        wuc = [sbuf(f"wuc{i}", [128, DT * 128], BF16, bs, side="right")
               for i in range(2)]
        wdc = [sbuf(f"wdc{i}", [128, D], BF16, bs, side="right")
               for i in range(6)]
        silh = sbuf("silh", [128, TOK], F32, bs, side="right")
        rcols = sbuf("rcols", [128, 8], F32, bs, side="right")
        scores = sbuf("scores", [128, E], F32, bs, side="right")
        emask = sbuf("emask", [128, E], F32, bs, side="right")
        ps_hg = psum("ps_hg", [128, 1024], bs)
        ps_hu = psum("ps_hu", [128, 1024], bs)
        ps_dn = [psum(f"ps_dn{i}", [128, 512], bs) for i in range(2)]
        ps_tr2 = [psum(f"ps_tr2{i}", [128, 512], bs) for i in range(2)]
        ps_r = ps_dn[0]

        bst = {"xin_free": None, "xnf_free": None,
               "tr_free": [None, None],
               "wg_free": [None, None], "wu_free": [None, None],
               "wd_free": [None] * 6, "hh_free": [None] * HT_E,
               "hg_free": None, "hu_free": None, "silh_free": None,
               "dn_free": [None, None]}
        hh_ev = [None] * HT_E

        for i in range(NT):
            xld = p.op(
                "g",
                lambda g, i=i: g.dma_start(
                    out=xin_b[:],
                    in_=ap["x_own"][i * 128:(i + 1) * 128, :]),
                waits=(bst["xin_free"],), inc="xinb", inc_k=16)
            p.op("v",
                 lambda v, i=i: v.tensor_scalar(
                     xn_fb[:], xin_b[:], m_col[:, i:i + 1],
                     rstd_col[:, i:i + 1], ALU.subtract, ALU.mult),
                 waits=(xld, bst["xnf_free"]))
            vev = p.op("v", lambda v: v.drain(), inc="v")
            bst["xin_free"] = vev
            for grp in range(4):
                b = grp % 2
                pev = p.op(
                    "t",
                    lambda t, grp=grp, b=b: [
                        t.matmul(ps_tr2[b][:, k * 128:(k + 1) * 128],
                                 xn_fb[:, (grp * 4 + k) * 128:
                                       (grp * 4 + k + 1) * 128],
                                 identb[:], start=True, stop=True)
                        for k in range(4)][-1],
                    waits=(vev, bst["tr_free"][b], ev_consts), inc="t")
                eev = p.op(
                    "v",
                    lambda v, grp=grp, b=b, i=i: v.tensor_copy(
                        xnT_b[:].rearrange("p (dt t) -> p dt t",
                                           dt=DT)[:, grp * 4:grp * 4 + 4,
                                                  i * 128:(i + 1) * 128],
                        ps_tr2[b][:, 0:512].rearrange(
                            "p (k c) -> p k c", k=4)) and v.drain(),
                    waits=(pev,), inc="v")
                bst["tr_free"][b] = eev
            bst["xnf_free"] = p.ev("t")
        xnT_b_ev = p.ev("v")

        def ffn_expert(wg_ap, wu_ap, wd_ap, nht, scale_cols):
            for ht in range(nht):
                b = ht % 2
                gld = p.op(
                    "g",
                    lambda g, wg_ap=wg_ap, ht=ht, b=b: g.dma_start(
                        out=wgc[b][:].rearrange("p (dt h) -> p dt h", dt=DT),
                        in_=wg_ap.rearrange("(dt p) h -> p dt h", p=128)
                            [:, :, ht * 128:(ht + 1) * 128]),
                    waits=(bst["wg_free"][b],), inc=f"wg{b}", inc_k=16)
                uld = p.op(
                    "g",
                    lambda g, wu_ap=wu_ap, ht=ht, b=b: g.dma_start(
                        out=wuc[b][:].rearrange("p (dt h) -> p dt h", dt=DT),
                        in_=wu_ap.rearrange("(dt p) h -> p dt h", p=128)
                            [:, :, ht * 128:(ht + 1) * 128]),
                    waits=(bst["wu_free"][b],), inc=f"wu{b}", inc_k=16)
                ev_gmm = p.op(
                    "t",
                    lambda t, b=b: [
                        t.matmul(ps_hg[:, tch * 512:(tch + 1) * 512],
                                 wgc[b][:, dt * 128:(dt + 1) * 128],
                                 xnT_b[:, (dt * NT + tch * 4) * 128:
                                       (dt * NT + tch * 4) * 128 + 512],
                                 start=(dt == 0), stop=(dt == DT - 1))
                        for tch in range(2) for dt in range(DT)][-1],
                    waits=(gld, xnT_b_ev, bst["hg_free"]), inc="t")
                ev_umm = p.op(
                    "t",
                    lambda t, b=b: [
                        t.matmul(ps_hu[:, tch * 512:(tch + 1) * 512],
                                 wuc[b][:, dt * 128:(dt + 1) * 128],
                                 xnT_b[:, (dt * NT + tch * 4) * 128:
                                       (dt * NT + tch * 4) * 128 + 512],
                                 start=(dt == 0), stop=(dt == DT - 1))
                        for tch in range(2) for dt in range(DT)][-1],
                    waits=(uld, bst["hu_free"]), inc="t")
                bst["wg_free"][b] = p.ev("t")
                bst["wu_free"][b] = p.ev("t")
                aev = p.op(
                    "a",
                    lambda a: [a.activation(silh[:, tch * 512:(tch + 1) * 512],
                                            ps_hg[:, tch * 512:(tch + 1) * 512],
                                            AF.Silu)
                               for tch in range(2)][-1] and a.drain(),
                    waits=(ev_gmm, bst["silh_free"]), inc="a")
                bst["hg_free"] = aev
                vev = p.op(
                    "v",
                    lambda v, ht=ht: [v.scalar_tensor_tensor(
                        hh[ht][:, tch * 512:(tch + 1) * 512],
                        ps_hu[:, tch * 512:(tch + 1) * 512], 1.0,
                        silh[:, tch * 512:(tch + 1) * 512],
                        ALU.mult, ALU.mult) for tch in range(2)][-1]
                    and v.drain(),
                    waits=(ev_umm, aev, bst["hh_free"][ht]), inc="v")
                bst["hu_free"] = vev
                bst["silh_free"] = vev
                hh_ev[ht] = vev
            kgroups = []
            s0 = 0
            while s0 < nht:
                kgroups.append((s0, min(s0 + 6, nht)))
                s0 += 6
            for (h0, h1) in kgroups:
                wd_ld = []
                for ht in range(h0, h1):
                    wd_ld.append(p.op(
                        "g",
                        lambda g, wd_ap=wd_ap, ht=ht, h0=h0: g.dma_start(
                            out=wdc[ht - h0][:],
                            in_=wd_ap[ht * 128:(ht + 1) * 128, :]),
                        waits=(bst["wd_free"][ht - h0],),
                        inc="wd", inc_k=16))
                for tt in range(NT):
                    for ddc in range(4):
                        slot = (tt * 4 + ddc) % 2
                        mmev = p.op(
                            "t",
                            lambda t, tt=tt, ddc=ddc, slot=slot,
                                   h0=h0, h1=h1: [
                                t.matmul(ps_dn[slot][:],
                                         hh[ht2][:, tt * 128:(tt + 1) * 128],
                                         wdc[ht2 - h0][:, ddc * 512:
                                                       (ddc + 1) * 512],
                                         start=(ht2 == h0),
                                         stop=(ht2 == h1 - 1))
                                for ht2 in range(h0, h1)][-1],
                            waits=tuple(hh_ev[h0:h1]) + tuple(wd_ld)
                                  + (bst["dn_free"][slot],),
                            inc="t")
                        if scale_cols is None:
                            vev = p.op(
                                "v",
                                lambda v, tt=tt, ddc=ddc, slot=slot:
                                v.tensor_add(
                                    acc[:, tt * D + ddc * 512:
                                        tt * D + (ddc + 1) * 512],
                                    acc[:, tt * D + ddc * 512:
                                        tt * D + (ddc + 1) * 512],
                                    ps_dn[slot][:]) and v.drain(),
                                waits=(mmev,), inc="v")
                        else:
                            vev = p.op(
                                "v",
                                lambda v, tt=tt, ddc=ddc, slot=slot,
                                       scale_cols=scale_cols:
                                v.scalar_tensor_tensor(
                                    acc[:, tt * D + ddc * 512:
                                        tt * D + (ddc + 1) * 512],
                                    ps_dn[slot][:], scale_cols(tt),
                                    acc[:, tt * D + ddc * 512:
                                        tt * D + (ddc + 1) * 512],
                                    ALU.mult, ALU.add) and v.drain(),
                                waits=(mmev,), inc="v")
                        bst["dn_free"][slot] = vev
                ge = p.ev("t")
                for k in range(6):
                    bst["wd_free"][k] = ge
            he = p.ev("t")
            for ht in range(nht):
                bst["hh_free"][ht] = he

        ffn_expert(ap["ws_gate"][:, 0:DE], ap["ws_up"][:, 0:DE],
                   ap["ws_down"][0:DE, :], HT_E, None)
        ffn_expert(ap["ws_gate"][:, DE:DS], ap["ws_up"][:, DE:DS],
                   ap["ws_down"][DE:DS, :], HT_E, None)

        r_free = None
        sc_free = None
        for tt in range(NT):
            mmev = p.op(
                "t",
                lambda t, tt=tt: [
                    t.matmul(ps_r[:, 0:E],
                             xnT_b[:, (dt * NT + tt) * 128:
                                   (dt * NT + tt) * 128 + 128],
                             wrb[:, dt * E:(dt + 1) * E],
                             start=(dt == 0), stop=(dt == DT - 1))
                    for dt in range(DT)][-1],
                waits=(ev_consts, xnT_b_ev, r_free, bst["dn_free"][0]),
                inc="t")
            mx = rcols[:, 0:1]
            sm = rcols[:, 1:2]
            rs = rcols[:, 2:3]
            m1 = rcols[:, 3:4]
            m2 = rcols[:, 4:5]
            vev = p.op("v", lambda v: v.reduce_max(
                    mx[:], ps_r[:, 0:E], axis=AX.X, negate=True) and v.drain(),
                waits=(mmev, sc_free), inc="v")
            aev = p.op("a", lambda a: a.activation(
                    scores[:], ps_r[:, 0:E], AF.Exp, bias=mx[:]) and a.drain(),
                waits=(vev,), inc="a")
            r_free = aev
            p.op("v", lambda v: v.reduce_sum(sm[:], scores[:], axis=AX.X),
                 waits=(aev,))
            p.op("v", lambda v: v.drain())
            p.op("v", lambda v: v.reciprocal(rs[:], sm[:]))
            p.op("v", lambda v: v.drain())
            p.op("v", lambda v: v.tensor_scalar(scores[:], scores[:], rs[:],
                                                None, ALU.mult))
            p.op("v", lambda v: v.drain())
            p.op("v", lambda v: v.reduce_max(m1[:], scores[:], axis=AX.X))
            p.op("v", lambda v: v.drain())
            p.op("v", lambda v: v.tensor_scalar(emask[:], scores[:], m1[:],
                                                None, ALU.is_ge))
            p.op("v", lambda v: v.drain())
            p.op("v", lambda v: v.scalar_tensor_tensor(
                    emask[:], emask[:], NEG, scores[:], ALU.mult, ALU.add))
            p.op("v", lambda v: v.drain())
            p.op("v", lambda v: v.reduce_max(m2[:], emask[:], axis=AX.X))
            p.op("v", lambda v: v.drain())
            p.op("v", lambda v: v.tensor_scalar(emask[:], scores[:], m2[:],
                                                None, ALU.is_ge))
            p.op("v", lambda v: v.drain())
            p.op("v", lambda v, tt=tt: v.tensor_mul(
                    combine[:, tt * E:(tt + 1) * E], emask[:], scores[:]))
            sc_free = p.op("v", lambda v: v.drain(), inc="v")
        bst["dn_free"][0] = r_free
        if "comb" in dbg:
            p.op("g", lambda sp: sp.dma_start(out=ap["dbg_comb"],
                                              in_=combine[:]),
                 waits=(p.ev("v"),), inc="dbg", inc_k=16)

        for e in range(E):
            ffn_expert(ap["we_gate"][e], ap["we_up"][e], ap["we_down"][e],
                       HT_E,
                       lambda tt, e=e: combine[:, tt * E + e: tt * E + e + 1])

        fin_free = None
        all_acc_done = p.ev("v")
        for i in range(NT):
            xld = p.op(
                "g",
                lambda g, i=i: g.dma_start(
                    out=xin_b[:],
                    in_=ap["x_own"][i * 128:(i + 1) * 128, :]),
                waits=(fin_free, bst["xin_free"], all_acc_done),
                inc="xinb", inc_k=16)
            p.op("v",
                 lambda v, i=i: v.tensor_scalar(
                     xn_fb[:], xin_b[:], m_col[:, i:i + 1],
                     rstd_col[:, i:i + 1], ALU.subtract, ALU.mult),
                 waits=(xld,))
            p.op("v", lambda v: v.drain())
            p.op("v", lambda v, i=i: v.tensor_add(
                    acc[:, i * D:(i + 1) * D], acc[:, i * D:(i + 1) * D],
                    xn_fb[:]))
            p.op("v", lambda v: v.drain())
            p.op("v", lambda v, i=i: v.tensor_add(
                    acc[:, i * D:(i + 1) * D], acc[:, i * D:(i + 1) * D],
                    xin_b[:]))
            vev = p.op("v", lambda v: v.drain(), inc="v")
            fin_free = vev
            bst["xin_free"] = vev
            p.op("s", lambda sp, i=i: sp.dma_start(
                    out=ap["y"][i * 128:(i + 1) * 128, :],
                    in_=acc[:, i * D:(i + 1) * D]),
                 waits=(vev,), inc="yst", inc_k=16)
        final_ev = p.ev("yst")
        p.op("s", lambda sp: None, waits=(final_ev,))
        p.flush()
        bs.close()

    return nc, ap



def make_inputs(full):
    x = np.asarray(full["L0_attn_x"], dtype=np.float32).reshape(S, D)
    w_router = np.asarray(full["L0_ffn_W_router"], dtype=np.float32)
    wrr = np.ascontiguousarray(
        w_router.reshape(DT, 128, E).transpose(1, 0, 2).reshape(128, DT * E))
    common = dict(
        x_prior=np.ascontiguousarray(x[0:PRIOR]),
        w_qkv=np.ascontiguousarray(full["L0_attn_w_qkv"], dtype=np.float32),
        w_swiglu=np.ascontiguousarray(full["L0_attn_w_swiglu"],
                                      dtype=np.float32),
        w_out=np.ascontiguousarray(full["L0_attn_w_out"], dtype=np.float32),
        w_router_r=wrr,
        we_gate=np.ascontiguousarray(full["L0_ffn_We_gate"], dtype=np.float32),
        we_up=np.ascontiguousarray(full["L0_ffn_We_up"], dtype=np.float32),
        we_down=np.ascontiguousarray(full["L0_ffn_We_down"],
                                     dtype=np.float32),
        ws_gate=np.ascontiguousarray(full["L0_ffn_Ws_gate"], dtype=np.float32),
        ws_up=np.ascontiguousarray(full["L0_ffn_Ws_up"], dtype=np.float32),
        ws_down=np.ascontiguousarray(full["L0_ffn_Ws_down"], dtype=np.float32),
        identb=np.eye(128, dtype=ml_dtypes.bfloat16),
        triu=np.triu(np.ones((128, 128))).astype(ml_dtypes.bfloat16),
        e0row=np.concatenate([np.ones((1, 128)), np.zeros((127, 128))],
                             axis=0).astype(np.float32),
        onesall=np.ones((128, 128), dtype=ml_dtypes.bfloat16),
        onescol_f=np.ones((128, 1), dtype=np.float32),
    )
    in_maps = []
    for c in range(NCORES):
        maskb = np.full((128, PRIOR // 128), NEG, dtype=np.float32)
        maskb[:, 0:NT * c] = 0.0
        m = dict(common)
        m["x_own"] = np.ascontiguousarray(x[c * TOK:(c + 1) * TOK])
        m["maskb"] = maskb
        in_maps.append(m)
    return in_maps


def kernel(**inputs):
    nc, _ = build_nc()
    in_maps = make_inputs(inputs)
    res = run_bass_kernel_spmd(nc, in_maps, core_ids=list(range(NCORES)))
    out = np.concatenate([res.results[c]["y"] for c in range(NCORES)], axis=0)
    return np.ascontiguousarray(out.reshape(1, S, D).astype(np.float32))
